# version 01
# baseline (speedup 1.0000x reference)
import numpy as np
from contextlib import ExitStack

import concourse.bass as bass
import concourse.mybir as mybir
from concourse.bass_utils import run_bass_kernel_spmd

F32 = mybir.dt.float32
BF16 = mybir.dt.bfloat16
I32 = mybir.dt.int32
ALU = mybir.AluOpType
AF = mybir.ActivationFunctionType
AX = mybir.AxisListType

NCORES = 8
D = 1024
NJ = 8
DI = 2048
NOT = 16
DFF = 4096
NFT = 32
NST = 16

DBG = False
DBGN = []

ENGINES = ("te", "ve", "se", "ge", "sy")
DMA_POOL = {"sy": [f"dsy{i}" for i in range(10)],
            "se": [f"dsc{i}" for i in range(10)],
            "ge": [f"dge{i}" for i in range(4)]}
SEMS = ("te", "ve", "se", "ge") + tuple(
    s for pool in DMA_POOL.values() for s in pool)


class Buf:
    __slots__ = ("ap", "wtok", "rtoks")

    def __init__(self, ap, wtok=None, rtoks=None):
        self.ap = ap
        self.wtok = wtok
        self.rtoks = dict(rtoks) if rtoks else {}


class Prog:
    def __init__(self):
        self.sched = {e: [] for e in ENGINES}
        self.cnt = {s: 0 for s in SEMS}
        self.hw = {}
        self.qidx = {q: 0 for q in DMA_POOL}
        self.qopen = {q: False for q in DMA_POOL}
        self.closed = {s: 0 for pool in DMA_POOL.values() for s in pool}
        self.fence = {q: None for q in DMA_POOL}

    def _close_group(self, q):
        pool = DMA_POOL[q]
        s = pool[self.qidx[q]]
        self.closed[s] = self.cnt[s]
        self.qopen[q] = False
        self.qidx[q] = (self.qidx[q] + 1) % len(pool)
        return s, self.closed[s]

    def _resolve_dma_wait(self, s, v):
        for q, pool in DMA_POOL.items():
            if s in pool:
                if self.closed[s] >= v and not (
                        self.qopen[q] and pool[self.qidx[q]] == s and
                        self.cnt[s] > self.closed[s]):
                    return s, self.closed[s] if self.closed[s] >= v else v
                if self.qopen[q] and pool[self.qidx[q]] == s:
                    self._close_group(q)
                    return s, self.closed[s]
                return s, max(self.closed[s], v)
        return s, v

    def op(self, eng, fn, ins=(), outs=(), inc=True, dma=False):
        toks = {}

        def add(t):
            if t is None:
                return
            s, v = t
            if toks.get(s, 0) < v:
                toks[s] = v
        for b in ins:
            add(b.wtok if isinstance(b, Buf) else b)
        for b in outs:
            add(b.wtok)
            for s, v in b.rtoks.items():
                add((s, v))
        waits = []
        for s, v in toks.items():
            if s.startswith("d"):
                s, v = self._resolve_dma_wait(s, v)
            if self.hw.get((eng, s), 0) >= v:
                continue
            self.hw[(eng, s)] = v
            waits.append((s, v))
        tok = None
        if inc:
            if dma:
                q = eng
                pool = DMA_POOL[q]
                if not self.qopen[q]:
                    s_new = pool[self.qidx[q]]
                    if self.cnt[s_new] > 0 and                             self.hw.get((eng, s_new), 0) < self.cnt[s_new]:
                        waits.append((s_new, self.cnt[s_new]))
                        self.hw[(eng, s_new)] = self.cnt[s_new]
                    self.qopen[q] = True
                sem = pool[self.qidx[q]]
                self.cnt[sem] += 16
                tok = (sem, self.cnt[sem])
            else:
                self.cnt[eng] += 1
                tok = (eng, self.cnt[eng])
        self.sched[eng].append((waits, fn, tok))
        if tok is not None:
            for b in ins:
                if isinstance(b, Buf):
                    s, v = tok
                    if b.rtoks.get(s, 0) < v:
                        b.rtoks[s] = v
            for b in outs:
                b.wtok = tok
                b.rtoks = {}
        return tok


def _replay(prog, eng_name, handle, sems):
    for waits, fn, tok in prog.sched[eng_name]:
        for s, v in waits:
            handle.wait_ge(sems[s], v)
        ins = fn(handle)
        if tok is not None:
            s, _ = tok
            ins.then_inc(sems[s], 16 if s.startswith("d") else 1)


class Arena:
    def __init__(self, handle, nbytes):
        self.h = handle
        self.size = nbytes
        self.cur = 0
        self.stack = [(0, [])]
        self.pending = {}

    def alloc(self, cols, dtype):
        esz = 2 if dtype == BF16 else 4
        off = (self.cur + esz - 1) // esz * esz
        assert off + cols * esz <= self.size, \
            f"arena overflow {off}+{cols * esz}>{self.size}"
        self.cur = off + cols * esz
        ap = self.h[:].bitcast(dtype)[:, off // esz: off // esz + cols]
        b = Buf(ap, rtoks=self.pending)
        self.stack[-1][1].append(b)
        return b

    def mark(self):
        self.stack.append((self.cur, []))

    def release(self):
        start, bufs = self.stack.pop()
        for b in bufs:
            if b.wtok:
                s, v = b.wtok
                if self.pending.get(s, 0) < v:
                    self.pending[s] = v
            for s, v in b.rtoks.items():
                if self.pending.get(s, 0) < v:
                    self.pending[s] = v
        self.cur = start


def _feat_cols(v, nt):
    return np.ascontiguousarray(np.asarray(v, np.float32).reshape(nt, 128).T)


COLNAMES = [("ln1_g", NJ), ("ln1_b", NJ), ("bq", NJ), ("bk", NJ), ("bv", NJ),
            ("bg", NJ), ("bo", NJ), ("ln2_g", NJ), ("ln2_b", NJ),
            ("b1", NFT), ("b2", NJ), ("cw0", NFT), ("cw1", NFT),
            ("cw2", NFT), ("cb", NFT), ("c1w0", NOT), ("c1w1", NOT),
            ("c1w2", NOT), ("c1w3", NOT), ("c1b", NOT), ("bdt", NOT),
            ("Dp", NOT), ("bias1f", NJ)]
WEIGHTS = [("Wq", 8, D, "bf"), ("Wk", 8, D, "bf"), ("Wv", 8, D, "bf"),
           ("Wg", 8, D, "bf"), ("Wo", 8, D, "bf"), ("W1", 32, D, "bf"),
           ("W2", 8, DFF, "bf"), ("w1T", 32, D, "f32"),
           ("w2T", 8, DFF, "f32"), ("W_in", 32, D, "bf"),
           ("W_dt", 16, 128, "bf"), ("W_out", 8, DI, "bf"),
           ("Kf", 8, D, "f32")]


def setup_core_inputs(inputs, T):
    L = 4 * T
    x = np.asarray(inputs["L0_attn_x"], np.float32)
    f32 = np.float32
    import ml_dtypes
    bf16 = ml_dtypes.bfloat16

    def wlay(w, bf=False):
        K, M = np.asarray(w).shape
        a = np.ascontiguousarray(
            np.asarray(w, f32).reshape(K, M // 128, 128).transpose(1, 0, 2))
        return np.ascontiguousarray(a.astype(bf16)) if bf else a

    g = {
        "Wq": wlay(inputs["L0_attn_Wq"], True),
        "Wk": wlay(inputs["L0_attn_Wk"], True),
        "Wv": wlay(inputs["L0_attn_Wv"], True),
        "Wg": wlay(inputs["L0_attn_Wg"], True),
        "Wo": wlay(np.asarray(inputs["L0_attn_Wo"], f32) * f32(0.125),
                   True),
        "W1": wlay(inputs["L0_attn_W1"], True),
        "W2": wlay(inputs["L0_attn_W2"], True),
        "w1T": wlay(np.ascontiguousarray(np.asarray(inputs["L0_ffn_w1"]).T)),
        "w2T": wlay(np.ascontiguousarray(np.asarray(inputs["L0_ffn_w2"]).T)),
        "W_in": wlay(inputs["L1_attn_W_in"], True),
        "W_dt": wlay(inputs["L1_attn_W_dt"], True),
        "W_out": wlay(inputs["L1_attn_W_out"], True),
        "Kf": wlay(inputs["L1_ffn_kernel"]),
    }
    Wx = np.ascontiguousarray(
        np.asarray(inputs["L1_attn_W_x"], f32).astype(bf16))
    alora = np.ascontiguousarray(
        np.asarray(inputs["L1_ffn_a"], f32).astype(bf16))
    blora = np.ascontiguousarray(
        np.asarray(inputs["L1_ffn_b"], f32).astype(bf16))
    cw0 = np.asarray(inputs["L0_ffn_conv_w"], f32)
    c1w = np.asarray(inputs["L1_attn_conv_w"], f32)
    cols = {
        "ln1_g": _feat_cols(inputs["L0_attn_ln1_g"], NJ),
        "ln1_b": _feat_cols(inputs["L0_attn_ln1_b"], NJ),
        "bq": _feat_cols(inputs["L0_attn_bq"], NJ),
        "bk": _feat_cols(inputs["L0_attn_bk"], NJ),
        "bv": _feat_cols(inputs["L0_attn_bv"], NJ),
        "bg": _feat_cols(inputs["L0_attn_bg"], NJ),
        "bo": _feat_cols(inputs["L0_attn_bo"], NJ),
        "ln2_g": _feat_cols(inputs["L0_attn_ln2_g"], NJ),
        "ln2_b": _feat_cols(inputs["L0_attn_ln2_b"], NJ),
        "b1": _feat_cols(inputs["L0_attn_b1"], NFT),
        "b2": _feat_cols(inputs["L0_attn_b2"], NJ),
        "cw0": _feat_cols(cw0[:, 0, 0], NFT),
        "cw1": _feat_cols(cw0[:, 0, 1], NFT),
        "cw2": _feat_cols(cw0[:, 0, 2], NFT),
        "cb": _feat_cols(inputs["L0_ffn_conv_b"], NFT),
        "c1w0": _feat_cols(c1w[:, 0], NOT),
        "c1w1": _feat_cols(c1w[:, 1], NOT),
        "c1w2": _feat_cols(c1w[:, 2], NOT),
        "c1w3": _feat_cols(c1w[:, 3], NOT),
        "c1b": _feat_cols(inputs["L1_attn_conv_b"], NOT),
        "bdt": _feat_cols(inputs["L1_attn_b_dt"], NOT),
        "Dp": _feat_cols(inputs["L1_attn_D_param"], NOT),
        "bias1f": _feat_cols(inputs["L1_ffn_bias"], NJ),
    }
    ident = np.eye(128, dtype=bf16)
    sel32 = np.zeros((32, 32 * 128), dtype=bf16)
    for n in range(32):
        sel32[n, n * 128:(n + 1) * 128] = 1
    dl = np.asarray(inputs["L0_attn_decay_logit"], f32)
    Al = np.asarray(inputs["L1_attn_A_log"], f32)

    ins = []
    for c in range(NCORES):
        b, qq = c // 4, c % 4
        m = {}
        m["x0"] = np.ascontiguousarray(x[b, qq * T:(qq + 1) * T].T)
        hl = np.zeros((2, D), f32)
        if qq * T - 1 >= 0:
            hl[0] = x[b, qq * T - 1]
        if (qq + 1) * T < L:
            hl[1] = x[b, (qq + 1) * T]
        m["x0h"] = np.ascontiguousarray(hl.T)
        for k, v in g.items():
            m[k] = v
        m["W_x"] = Wx
        m["a_lora"] = alora
        m["b_lora"] = blora
        for k, v in cols.items():
            m["c_" + k] = v
        dcols = np.zeros((128, 1), f32)
        for p_ in range(128):
            dcols[p_, 0] = dl[(c * 128 + p_) // 64]
        m["my_decay"] = dcols
        acols = np.zeros((128, 2 * NST), f32)
        for oi in range(2):
            go = 2 * c + oi
            acols[:, oi * NST:(oi + 1) * NST] = Al[go * 128:(go + 1) * 128]
        m["my_Alog"] = np.ascontiguousarray(acols)
        hm = np.zeros((128, 4), f32)
        if qq > 0:
            hm[:, qq - 1] = 1.0
        m["halo_mask"] = hm
        m["ident"] = ident
        m["sel32"] = sel32
        m["sgn_bc"] = np.ones((32, 1), f32)
        ins.append(m)
    return ins


def build_program(T):
    TC = T // 4
    L = 4 * T
    nkT = T // 128
    nc = bass.Bass("TRN2", target_bir_lowering=False)
    din = {}

    def dr(name, shape, dt=F32):
        din[name] = nc.dram_tensor(name, list(shape), dt,
                                   kind="ExternalInput").ap()

    dr("x0", (D, T))
    dr("x0h", (D, 2))
    for w, nb, K, dt_ in WEIGHTS:
        dr(w, (nb, K, 128), BF16 if dt_ == "bf" else F32)
    dr("W_x", (DI, 160), BF16)
    dr("a_lora", (D, 16), BF16)
    dr("b_lora", (16, D), BF16)
    for k, nt in COLNAMES:
        dr("c_" + k, (128, nt))
    dr("my_decay", (128, 1))
    dr("my_Alog", (128, 2 * NST))
    dr("halo_mask", (128, 4))
    dr("ident", (128, 128), BF16)
    dr("sel32", (32, 32 * 128), BF16)
    dr("sgn_bc", (32, 1))

    xout_d = nc.dram_tensor("xout", [D, T], F32, kind="ExternalOutput").ap()
    dbg_d = {}
    if DBG:
        for nm, shape in [("d_xn0", (D, T)), ("d_y1", (D, T)),
                          ("d_kv", (D, T)), ("d_qg", (D, T)),
                          ("d_outs", (D, T)), ("d_x2", (D, T)),
                          ("d_s0", (D, T)), ("d_qin", (D, T)),
                          ("d_h", (DFF, T)), ("d_hc", (DFF, T)),
                          ("d_g0", (D, T)), ("d_x1", (D, T)),
                          ("d_uc", (DI, T)), ("d_dt", (DI, T)),
                          ("d_bc", (32, T)), ("d_y", (DI, T)),
                          ("d_g1", (D, T)), ("d_s1", (D, T))]:
            if nm in DBGN or "all" in DBGN:
                dbg_d[nm] = nc.dram_tensor(nm, list(shape), F32,
                                           kind="ExternalOutput").ap()

    def idr(name, shape, dt=F32):
        return Buf(nc.dram_tensor(name, list(shape), dt).ap())

    sc_s0 = idr("sc_s0", (NJ, 128, T))
    sc_hc = idr("sc_hc", (NFT, 128, T), BF16)
    sc_z = idr("sc_z", (NOT, 128, T), BF16)
    sc_x1 = idr("sc_x1", (NJ, 128, T))
    a0_in = idr("a0_in", (8, 2, 128, T), BF16)
    a0_out = idr("a0_out", (8, 2, 128, T), BF16)
    a0b_in = idr("a0b_in", (8, 128, T), BF16)
    a0b_out = idr("a0b_out", (8, 128, T), BF16)
    a1_in = idr("a1_in", (8, 2, 2, 128, T), BF16)
    a1_out = idr("a1_out", (8, 2, 2, 128, T), BF16)
    a1b_in = idr("a1b_in", (8, 2, 128, T), BF16)
    a1b_out = idr("a1b_out", (8, 2, 128, T), BF16)
    agbc_in = idr("agbc_in", (32, T), BF16)
    agbc_out = idr("agbc_out", (8, 32, T), BF16)
    agh_in = idr("agh_in", (NJ, 128, 3))
    agh_out = idr("agh_out", (4, NJ, 128, 3))
    RG = [[0, 1, 2, 3], [4, 5, 6, 7]]
    RG8 = [[0, 1, 2, 3, 4, 5, 6, 7]]

    p = Prog()
    es = ExitStack()
    with es:
        AR_h = es.enter_context(nc.sbuf_tensor("AR", [128, 152 * 256], F32))
        AR = Arena(AR_h, 152 * 1024)
        C_h = es.enter_context(nc.sbuf_tensor("CR", [128, 14 * 256], F32))
        CR = Arena(C_h, 14 * 1024)
        RW_h = es.enter_context(nc.sbuf_tensor("RW", [128, 5 * T], F32))
        RS = [Buf(RW_h[0:1, i * T:(i + 1) * T]) for i in range(5)]

        PS = [Buf(es.enter_context(
            nc.psum_tensor(f"PS{i}", [128, 512], F32)).ap())
            for i in range(8)]
        sems = {s: es.enter_context(nc.semaphore("s_" + s)) for s in SEMS}

        def dma(stream, out, in_, outs=(), ins=()):
            return p.op(stream, lambda e, o=out, i=in_:
                        e.dma_start(out=o, in_=i),
                        ins=ins, outs=outs, dma=True)

        def mm(lhs_list, rhs_list, out_ap, ins=(), outs=(), accum=False,
               final=True):
            n = len(lhs_list)
            tok = None
            for i in range(n):
                first = (i == 0) and not accum
                stop = final and (i == n - 1)
                tok = p.op("te",
                           lambda e, l=lhs_list[i], r=rhs_list[i], st=first,
                           sp=stop: e.matmul(out=out_ap, lhsT=l, rhs=r,
                                             start=st, stop=sp),
                           ins=(tuple(ins) + tuple(
                               b.wtok for b in outs if b.wtok) + tuple(
                               (s, v) for b in outs
                               for s, v in b.rtoks.items()))
                           if (i == 0 or i == n - 1) else (),
                           outs=outs if i == n - 1 else (),
                           inc=(i == n - 1))
            return tok

        def ve(fn, ins=(), outs=()):
            return p.op("ve", fn, ins=ins, outs=outs)

        def ge_(fn, ins=(), outs=()):
            return p.op("ge", fn, ins=ins, outs=outs)

        def se(fn, ins=(), outs=()):
            return p.op("se", fn, ins=ins, outs=outs)

        def te(fn, ins=(), outs=()):
            return p.op("te", fn, ins=ins, outs=outs)

        from concourse.aot_env import allow_ici_all_to_all

        def cc(kind, inb, outb, rg=None):
            rgv = rg if rg is not None else RG
            def emit(e, k=kind, a=inb, b=outb, rgv=rgv):
                with allow_ici_all_to_all():
                    return e.collective_compute(
                        k, ALU.bypass, replica_groups=rgv,
                        ins=[a.ap.opt()], outs=[b.ap.opt()])
            return p.op("ge", emit, ins=(inb,), outs=(outb,))

        ones_c = CR.alloc(128, F32)
        onesb_col = CR.alloc(1, BF16)
        ident_b = CR.alloc(128, BF16)
        sel32 = CR.alloc(32 * 128, BF16)
        ve(lambda e: e.memset(ones_c.ap[0:1, :], 1.0), outs=(ones_c,))
        ve(lambda e: e.memset(onesb_col.ap[:, 0:1], 1.0), outs=(onesb_col,))
        dma("sy", ident_b.ap[:], din["ident"], outs=(ident_b,))
        dma("sy", sel32.ap[0:32, :], din["sel32"], outs=(sel32,))
        cols_t = {}
        for k, nt in COLNAMES:
            cols_t[k] = CR.alloc(nt, F32)
            dma("sy", cols_t[k].ap[:], din["c_" + k], outs=(cols_t[k],))
        myA = CR.alloc(2 * NST, F32)
        negA = CR.alloc(2 * NST, F32)
        mydec = CR.alloc(1, F32)
        halo_m = CR.alloc(4, F32)
        sgnbc = CR.alloc(1, F32)
        dma("sy", sgnbc.ap[0:32, 0:1], din["sgn_bc"], outs=(sgnbc,))
        dma("sy", myA.ap[:], din["my_Alog"], outs=(myA,))
        dma("sy", mydec.ap[:], din["my_decay"], outs=(mydec,))
        dma("sy", halo_m.ap[:], din["halo_mask"], outs=(halo_m,))
        se(lambda e: e.activation(out=negA.ap[:], in_=myA.ap[:],
                                  func=AF.Exp), ins=(myA,), outs=(negA,))
        ve(lambda e: e.tensor_scalar(out=negA.ap[:], in0=negA.ap[:],
                                     scalar1=-1.0, scalar2=None,
                                     op0=ALU.mult), ins=(negA,),
           outs=(negA,))
        ident_f = CR.alloc(128, F32)
        ve(lambda e: e.tensor_copy(out=ident_f.ap[:], in_=ident_b.ap[:]),
           ins=(ident_b,), outs=(ident_f,))
        mxsb = CR.alloc(128, F32)
        mxc = CR.alloc(nkT + 1, F32)
        x0h_t = CR.alloc(2 * NJ, F32)
        for j in range(NJ):
            dma("sy", x0h_t.ap[:, 2 * j:2 * j + 2],
                din["x0h"][j * 128:(j + 1) * 128, :], outs=(x0h_t,))

        def srow(w):
            return CR.alloc(w, F32)

        def dbg_dump(name, tiles, width=T, scale=None):
            if not (DBG and name in dbg_d):
                return
            AR.mark()
            tmp = AR.alloc(width, F32)
            for j, bb in enumerate(tiles):
                if scale is None:
                    ve(lambda e, bb=bb: e.tensor_copy(
                        out=tmp.ap[:, 0:width], in_=bb.ap[:, 0:width]),
                       ins=(bb,), outs=(tmp,))
                else:
                    ve(lambda e, bb=bb: e.tensor_scalar(
                        out=tmp.ap[:, 0:width], in0=bb.ap[:, 0:width],
                        scalar1=scale, scalar2=None, op0=ALU.mult),
                       ins=(bb,), outs=(tmp,))
                dma("se", dbg_d[name][j * 128:(j + 1) * 128, 0:width],
                    tmp.ap[:, 0:width], ins=(tmp,))
            AR.release()

        def ln_stats(fill_bf, nt, width, mu_b, r_b):
            AR.mark()
            xb = AR.alloc(512, BF16)
            xs = AR.alloc(512, BF16)
            ntc = (width + 511) // 512
            for t in range(ntc):
                w = min(512, width - t * 512)
                for j in range(nt):
                    fill_bf(j, t * 512, w, xb)
                    se(lambda e, w=w: e.activation(
                        out=xs.ap[:, 0:w], in_=xb.ap[:, 0:w],
                        func=AF.Square), ins=(xb,), outs=(xs,))
                    mm([onesb_col.ap[:, 0:1]], [xb.ap[:, 0:w]],
                       PS[6].ap[0:1, 0:w], ins=(xb, onesb_col),
                       outs=(PS[6],), accum=(j > 0), final=(j == nt - 1))
                    mm([onesb_col.ap[:, 0:1]], [xs.ap[:, 0:w]],
                       PS[7].ap[0:1, 0:w], ins=(xs, onesb_col),
                       outs=(PS[7],), accum=(j > 0), final=(j == nt - 1))
                se(lambda e, t=t, w=w: e.copy(
                    out=mu_b.ap[0:1, t * 512:t * 512 + w],
                    in_=PS[6].ap[0:1, 0:w]), ins=(PS[6],), outs=(mu_b,))
                se(lambda e, t=t, w=w: e.copy(
                    out=r_b.ap[0:1, t * 512:t * 512 + w],
                    in_=PS[7].ap[0:1, 0:w]), ins=(PS[7],), outs=(r_b,))
            nd = float(nt * 128)
            ve(lambda e: e.tensor_scalar(out=mu_b.ap[0:1, 0:width],
                                         in0=mu_b.ap[0:1, 0:width],
                                         scalar1=1.0 / nd, scalar2=None,
                                         op0=ALU.mult), ins=(mu_b,),
               outs=(mu_b,))
            msq = AR.alloc(width, F32)
            se(lambda e: e.activation(out=msq.ap[0:1, 0:width],
                                      in_=mu_b.ap[0:1, 0:width],
                                      func=AF.Square), ins=(mu_b,),
               outs=(msq,))
            ve(lambda e: e.tensor_scalar(out=r_b.ap[0:1, 0:width],
                                         in0=r_b.ap[0:1, 0:width],
                                         scalar1=1.0 / nd, scalar2=None,
                                         op0=ALU.mult), ins=(r_b,),
               outs=(r_b,))
            ve(lambda e: e.tensor_tensor(out=r_b.ap[0:1, 0:width],
                                         in0=r_b.ap[0:1, 0:width],
                                         in1=msq.ap[0:1, 0:width],
                                         op=ALU.subtract),
               ins=(r_b, msq), outs=(r_b,))
            ve(lambda e: e.tensor_scalar(out=r_b.ap[0:1, 0:width],
                                         in0=r_b.ap[0:1, 0:width],
                                         scalar1=1e-5, scalar2=None,
                                         op0=ALU.add), ins=(r_b,),
               outs=(r_b,))
            se(lambda e: e.activation(out=r_b.ap[0:1, 0:width],
                                      in_=r_b.ap[0:1, 0:width],
                                      func=AF.Sqrt), ins=(r_b,),
               outs=(r_b,))
            ve(lambda e: e.reciprocal(out=r_b.ap[0:1, 0:width],
                                      in_=r_b.ap[0:1, 0:width]),
               ins=(r_b,), outs=(r_b,))
            AR.release()

        def bcast(row_buf, lo, w, ps_buf):
            mm([ones_c.ap[0:1, :]], [row_buf.ap[0:1, lo:lo + w]],
               ps_buf.ap[:, 0:w], ins=(row_buf, ones_c), outs=(ps_buf,))

        def wslab(dramname, blk, ncols, dst):
            nkc = ncols // 128
            dma("sy",
                dst.ap[:, 0:ncols].rearrange("p (kc c) -> p kc c",
                                             kc=nkc),
                din[dramname][blk].rearrange("(kc p) c -> p kc c", p=128),
                outs=(dst,))

        mu0, r0 = RS[0], RS[1]
        mu0h = srow(2)
        r0h = srow(2)
        AR.mark()
        KV = [AR.alloc(T, BF16) for _ in range(NJ)]
        QG = [AR.alloc(T, BF16) for _ in range(NJ)]
        AR.mark()
        Y1 = [AR.alloc(T, BF16) for _ in range(NJ)]
        AR.mark()
        XN0 = [AR.alloc(T, BF16) for _ in range(NJ)]
        xf = AR.alloc(512, F32)

        def fill_from_x0(j, lo, w, xb):
            dma("sy", xf.ap[:, 0:w],
                din["x0"][j * 128:(j + 1) * 128, lo:lo + w], outs=(xf,))
            ve(lambda e, w=w: e.tensor_copy(out=xb.ap[:, 0:w],
                                            in_=xf.ap[:, 0:w]),
               ins=(xf,), outs=(xb,))
        ln_stats(fill_from_x0, NJ, T, mu0, r0)

        def fill_from_cr(src, percol=2):
            def f(j, lo, w, xb):
                ve(lambda e, j=j, lo=lo, w=w: e.tensor_copy(
                    out=xb.ap[:, 0:w],
                    in_=src.ap[:, percol * j + lo:percol * j + lo + w]),
                   ins=(src,), outs=(xb,))
            return f
        ln_stats(fill_from_cr(x0h_t), NJ, 2, mu0h, r0h)

        for t in range(4):
            bcast(mu0, t * TC, TC, PS[6])
            bcast(r0, t * TC, TC, PS[7])
            AR.mark()
            tmpa = AR.alloc(TC, F32)
            for j in range(NJ):
                dma("sy", tmpa.ap[:],
                    din["x0"][j * 128:(j + 1) * 128, t * TC:(t + 1) * TC],
                    outs=(tmpa,))
                ve(lambda e: e.tensor_tensor(
                    out=tmpa.ap[:], in0=tmpa.ap[:], in1=PS[6].ap[:, 0:TC],
                    op=ALU.subtract), ins=(tmpa, PS[6]), outs=(tmpa,))
                ve(lambda e, j=j, t=t: e.tensor_tensor(
                    out=XN0[j].ap[:, t * TC:(t + 1) * TC], in0=tmpa.ap[:],
                    in1=PS[7].ap[:, 0:TC], op=ALU.mult),
                   ins=(tmpa, PS[7]), outs=(XN0[j],))
            AR.release()
        dbg_dump("d_xn0", XN0)
        mu1, r1 = RS[2], RS[3]

        def fill_from_tiles(tiles):
            def f(j, lo, w, xb):
                ve(lambda e, j=j, lo=lo, w=w: e.tensor_copy(
                    out=xb.ap[:, 0:w], in_=tiles[j].ap[:, lo:lo + w]),
                   ins=(tiles[j],), outs=(xb,))
            return f
        ln_stats(fill_from_tiles(XN0), NJ, T, mu1, r1)
        for t in range(4):
            bcast(mu1, t * TC, TC, PS[6])
            bcast(r1, t * TC, TC, PS[7])
            AR.mark()
            tmpb = AR.alloc(TC, F32)
            for j in range(NJ):
                ve(lambda e, j=j, t=t: e.tensor_tensor(
                    out=tmpb.ap[:], in0=XN0[j].ap[:, t * TC:(t + 1) * TC],
                    in1=PS[6].ap[:, 0:TC], op=ALU.subtract),
                   ins=(XN0[j], PS[6]), outs=(tmpb,))
                ve(lambda e: e.tensor_tensor(out=tmpb.ap[:], in0=tmpb.ap[:],
                                             in1=PS[7].ap[:, 0:TC],
                                             op=ALU.mult),
                   ins=(tmpb, PS[7]), outs=(tmpb,))
                ve(lambda e, j=j, t=t: e.tensor_scalar(
                    out=Y1[j].ap[:, t * TC:(t + 1) * TC], in0=tmpb.ap[:],
                    scalar1=cols_t["ln1_g"].ap[:, j:j + 1],
                    scalar2=cols_t["ln1_b"].ap[:, j:j + 1],
                    op0=ALU.mult, op1=ALU.add),
                   ins=(tmpb, cols_t["ln1_g"], cols_t["ln1_b"]),
                   outs=(Y1[j],))
            AR.release()
        AR.release()
        dbg_dump("d_y1", Y1)

        AR.mark()
        slq = AR.alloc(1024, BF16)
        slk = AR.alloc(1024, BF16)
        slv = AR.alloc(1024, BF16)
        slg = AR.alloc(1024, BF16)
        kf = AR.alloc(TC, F32)
        vf = AR.alloc(TC, F32)
        qf = AR.alloc(TC, F32)
        gf = AR.alloc(TC, F32)
        for j in range(NJ):
            for w, dst in [("Wq", slq), ("Wk", slk), ("Wv", slv),
                           ("Wg", slg)]:
                wslab(w, j, 1024, dst)
            for t in range(4):
                for (sl, psi, bias, dst, func) in [
                        (slk, 0, "bk", kf, AF.Identity),
                        (slv, 1, "bv", vf, AF.Identity),
                        (slq, 2, "bq", qf, AF.Identity),
                        (slg, 3, "bg", gf, AF.Sigmoid)]:
                    mm([sl.ap[:, kc * 128:(kc + 1) * 128]
                        for kc in range(8)],
                       [Y1[kc].ap[:, t * TC:(t + 1) * TC]
                        for kc in range(8)],
                       PS[psi].ap[:, 0:TC], ins=[sl] + Y1,
                       outs=(PS[psi],))
                    se(lambda e, psi=psi, dst=dst, bias=bias, func=func,
                       j=j: e.activation(out=dst.ap[:],
                                         in_=PS[psi].ap[:, 0:TC],
                                         func=func,
                                         bias=cols_t[bias].ap[:, j:j + 1]),
                       ins=(PS[psi], cols_t[bias]), outs=(dst,))
                ve(lambda e, j=j, t=t: e.tensor_tensor(
                    out=KV[j].ap[:, t * TC:(t + 1) * TC], in0=kf.ap[:],
                    in1=vf.ap[:], op=ALU.mult), ins=(kf, vf),
                   outs=(KV[j],))
                ve(lambda e, j=j, t=t: e.tensor_tensor(
                    out=QG[j].ap[:, t * TC:(t + 1) * TC], in0=qf.ap[:],
                    in1=gf.ap[:], op=ALU.mult), ins=(qf, gf),
                   outs=(QG[j],))
        AR.release()
        AR.release()
        dbg_dump("d_kv", KV)
        dbg_dump("d_qg", QG)
        for sl_ in range(8):
            dma("se", a0_in.ap[sl_, 0], KV[sl_].ap[:],
                ins=(KV[sl_],), outs=(a0_in,))
            dma("se", a0_in.ap[sl_, 1], QG[sl_].ap[:],
                ins=(QG[sl_],), outs=(a0_in,))
        cc("AllToAll", a0_in, a0_out, rg=RG8)
        AR.release()

        AR.mark()
        dec2 = CR.alloc(1, F32)
        se(lambda e: e.activation(out=dec2.ap[:], in_=mydec.ap[:],
                                  func=AF.Sigmoid), ins=(mydec,),
           outs=(dec2,))
        kv_s = AR.alloc(L, BF16)
        qg_s = AR.alloc(L, BF16)
        out_s = AR.alloc(L, BF16)
        CW = min(2048, L)
        NQ = L // CW
        onesq = AR.alloc(CW, F32)
        decbc = AR.alloc(CW, F32)
        stb = [AR.alloc(CW, F32) for _ in range(2)]
        ve(lambda e: e.memset(onesq.ap[:], 1.0), outs=(onesq,))
        ve(lambda e: e.tensor_scalar(
            out=decbc.ap[:], in0=onesq.ap[:],
            scalar1=dec2.ap[:, 0:1], scalar2=None, op0=ALU.mult),
           ins=(onesq, dec2), outs=(decbc,))
        for b_ in range(2):
            for r in range(4):
                dma("sy", kv_s.ap[:, r * T:(r + 1) * T],
                    a0_out.ap[4 * b_ + r, 0], ins=(a0_out,), outs=(kv_s,))
                dma("sy", qg_s.ap[:, r * T:(r + 1) * T],
                    a0_out.ap[4 * b_ + r, 1], ins=(a0_out,), outs=(qg_s,))
            for q_ in range(NQ):
                dst = stb[q_ % 2]
                init = 0.0 if q_ == 0 else stb[(q_ - 1) % 2].ap[:,
                                                                CW - 1:CW]
                extra = () if q_ == 0 else (stb[(q_ - 1) % 2],)
                ve(lambda e, q_=q_, dst=dst, init=init:
                   e.tensor_tensor_scan(
                       out=dst.ap[:], data0=decbc.ap[:],
                       data1=kv_s.ap[:, q_ * CW:(q_ + 1) * CW],
                       initial=init, op0=ALU.mult, op1=ALU.add),
                   ins=(decbc, kv_s) + extra, outs=(dst,))
                ve(lambda e, q_=q_, dst=dst: e.tensor_tensor(
                    out=out_s.ap[:, q_ * CW:(q_ + 1) * CW],
                    in0=qg_s.ap[:, q_ * CW:(q_ + 1) * CW],
                    in1=dst.ap[:], op=ALU.mult),
                   ins=(qg_s, dst), outs=(out_s,))
            for r in range(4):
                dma("se", a0b_in.ap[4 * b_ + r],
                    out_s.ap[:, r * T:(r + 1) * T],
                    ins=(out_s,), outs=(a0b_in,))
        cc("AllToAll", a0b_in, a0b_out, rg=RG8)
        AR.release()

        AR.mark()
        S2 = [AR.alloc(T, F32) for _ in range(NJ)]
        AR.mark()
        OUTS = [AR.alloc(T, BF16) for _ in range(NJ)]
        for j in range(NJ):
            dma("sy", OUTS[j].ap[:], a0b_out.ap[j],
                ins=(a0b_out,), outs=(OUTS[j],))
        dbg_dump("d_outs", OUTS, scale=0.125)
        slab = AR.alloc(1024, BF16)
        xt = AR.alloc(TC, F32)
        for j in range(NJ):
            wslab("Wo", j, 1024, slab)
            for t in range(4):
                mm([slab.ap[:, kc * 128:(kc + 1) * 128]
                    for kc in range(8)],
                   [OUTS[kc].ap[:, t * TC:(t + 1) * TC]
                    for kc in range(8)],
                   PS[t % 2].ap[:, 0:TC], ins=[slab] + OUTS,
                   outs=(PS[t % 2],))
                bcast(mu0, t * TC, TC, PS[6])
                bcast(r0, t * TC, TC, PS[7])
                dma("sy", xt.ap[:],
                    din["x0"][j * 128:(j + 1) * 128, t * TC:(t + 1) * TC],
                    outs=(xt,))
                ve(lambda e: e.tensor_tensor(out=xt.ap[:], in0=xt.ap[:],
                                             in1=PS[6].ap[:, 0:TC],
                                             op=ALU.subtract),
                   ins=(xt, PS[6]), outs=(xt,))
                ve(lambda e: e.tensor_tensor(out=xt.ap[:], in0=xt.ap[:],
                                             in1=PS[7].ap[:, 0:TC],
                                             op=ALU.mult),
                   ins=(xt, PS[7]), outs=(xt,))
                ve(lambda e, j=j, t=t, ps=PS[t % 2]:
                   e.scalar_tensor_tensor(
                       out=S2[j].ap[:, t * TC:(t + 1) * TC],
                       in0=ps.ap[:, 0:TC],
                       scalar=cols_t["bo"].ap[:, j:j + 1], in1=xt.ap[:],
                       op0=ALU.add, op1=ALU.add),
                   ins=(PS[t % 2], cols_t["bo"], xt), outs=(S2[j],))
        AR.release()
        dbg_dump("d_x2", S2)
        mu2, r2 = RS[2], RS[3]
        ln_stats(fill_from_tiles(S2), NJ, T, mu2, r2)
        for t in range(4):
            AR.mark()
            h2b = [AR.alloc(TC, BF16) for _ in range(NJ)]
            bcast(mu2, t * TC, TC, PS[6])
            bcast(r2, t * TC, TC, PS[7])
            tmpc = AR.alloc(TC, F32)
            for j in range(NJ):
                ve(lambda e, j=j, t=t: e.tensor_tensor(
                    out=tmpc.ap[:], in0=S2[j].ap[:, t * TC:(t + 1) * TC],
                    in1=PS[6].ap[:, 0:TC], op=ALU.subtract),
                   ins=(S2[j], PS[6]), outs=(tmpc,))
                ve(lambda e: e.tensor_tensor(out=tmpc.ap[:], in0=tmpc.ap[:],
                                             in1=PS[7].ap[:, 0:TC],
                                             op=ALU.mult),
                   ins=(tmpc, PS[7]), outs=(tmpc,))
                ve(lambda e, j=j: e.tensor_scalar(
                    out=h2b[j].ap[:], in0=tmpc.ap[:],
                    scalar1=cols_t["ln2_g"].ap[:, j:j + 1],
                    scalar2=cols_t["ln2_b"].ap[:, j:j + 1],
                    op0=ALU.mult, op1=ALU.add),
                   ins=(tmpc, cols_t["ln2_g"], cols_t["ln2_b"]),
                   outs=(h2b[j],))
            ff1 = [AR.alloc(TC, BF16) for _ in range(NFT)]
            sla3 = AR.alloc(4096, BF16)
            for ot in range(NFT):
                wslab("W1", ot, 1024, sla3)
                mm([sla3.ap[:, kc * 128:(kc + 1) * 128]
                    for kc in range(8)],
                   [h2b[kc].ap[:] for kc in range(8)],
                   PS[ot % 4].ap[:, 0:TC], ins=[sla3] + h2b,
                   outs=(PS[ot % 4],))
                se(lambda e, ot=ot, ps=PS[ot % 4]: e.activation(
                    out=ff1[ot].ap[:], in_=ps.ap[:, 0:TC], func=AF.Gelu,
                    bias=cols_t["b1"].ap[:, ot:ot + 1]),
                   ins=(PS[ot % 4], cols_t["b1"]), outs=(ff1[ot],))
            for j in range(NJ):
                wslab("W2", j, 4096, sla3)
                mm([sla3.ap[:, kc * 128:(kc + 1) * 128]
                    for kc in range(32)],
                   [ff1[kc].ap[:] for kc in range(32)],
                   PS[4 + j % 2].ap[:, 0:TC], ins=[sla3] + ff1,
                   outs=(PS[4 + j % 2],))
                ve(lambda e, j=j, t=t, ps=PS[4 + j % 2]:
                   e.scalar_tensor_tensor(
                       out=S2[j].ap[:, t * TC:(t + 1) * TC],
                       in0=ps.ap[:, 0:TC],
                       scalar=cols_t["b2"].ap[:, j:j + 1],
                       in1=S2[j].ap[:, t * TC:(t + 1) * TC],
                       op0=ALU.add, op1=ALU.add),
                   ins=(PS[4 + j % 2], cols_t["b2"], S2[j]), outs=(S2[j],))
            AR.release()
        dbg_dump("d_s0", S2)
        for j in range(NJ):
            dma("se", sc_s0.ap[j], S2[j].ap[:], ins=(S2[j],),
                outs=(sc_s0,))
        AR.release()

        def stream_reduce_scalar(dramname, nblk, K, op, postmean=None):
            AR.mark()
            wf3 = AR.alloc(4096, F32)
            acc = AR.alloc(1, F32)
            red = AR.alloc(1, F32)
            for blk in range(nblk):
                wslab(dramname, blk, K, wf3)
                ve(lambda e, K=K: e.tensor_reduce(
                    out=red.ap[:, 0:1], in_=wf3.ap[:, 0:K], axis=AX.X,
                    op=op, apply_absolute_value=True),
                   ins=(wf3,), outs=(red,))
                if blk == 0:
                    ve(lambda e: e.tensor_copy(out=acc.ap[:, 0:1],
                                               in_=red.ap[:, 0:1]),
                       ins=(red,), outs=(acc,))
                else:
                    ve(lambda e, op=op: e.tensor_tensor(
                        out=acc.ap[:, 0:1], in0=acc.ap[:, 0:1],
                        in1=red.ap[:, 0:1], op=op),
                       ins=(red, acc), outs=(acc,))
            te(lambda e: e.transpose(out=PS[5].ap[0:1, 0:128],
                                     in_=acc.ap[:, 0:1],
                                     identity=ident_f.ap[:]),
               ins=(acc, ident_f), outs=(PS[5],))
            out_row = srow(1)
            if op == ALU.add:
                ve(lambda e: e.tensor_reduce(
                    out=out_row.ap[0:1, 0:1], in_=PS[5].ap[0:1, 0:128],
                    axis=AX.X, op=ALU.add), ins=(PS[5],),
                   outs=(out_row,))
                ve(lambda e, pm=postmean: e.tensor_scalar(
                    out=out_row.ap[0:1, 0:1], in0=out_row.ap[0:1, 0:1],
                    scalar1=pm, scalar2=1e-5, op0=ALU.mult, op1=ALU.max),
                   ins=(out_row,), outs=(out_row,))
            else:
                ve(lambda e: e.tensor_reduce(
                    out=out_row.ap[0:1, 0:1], in_=PS[5].ap[0:1, 0:128],
                    axis=AX.X, op=ALU.max), ins=(PS[5],),
                   outs=(out_row,))
            AR.release()
            return out_row

        def col_of(row_buf, scale=1.0, recip=False):
            tmpr = srow(1)
            if recip:
                ve(lambda e: e.reciprocal(out=tmpr.ap[0:1, 0:1],
                                          in_=row_buf.ap[0:1, 0:1]),
                   ins=(row_buf,), outs=(tmpr,))
            else:
                ve(lambda e: e.tensor_copy(out=tmpr.ap[0:1, 0:1],
                                           in_=row_buf.ap[0:1, 0:1]),
                   ins=(row_buf,), outs=(tmpr,))
            if scale != 1.0:
                ve(lambda e, s=scale: e.tensor_scalar(
                    out=tmpr.ap[0:1, 0:1], in0=tmpr.ap[0:1, 0:1],
                    scalar1=s, scalar2=None, op0=ALU.mult),
                   ins=(tmpr,), outs=(tmpr,))
            mm([ones_c.ap[0:1, :]], [tmpr.ap[0:1, 0:1]],
               PS[5].ap[:, 0:1], ins=(tmpr, ones_c), outs=(PS[5],))
            colb = CR.alloc(1, F32)
            se(lambda e: e.copy(out=colb.ap[:, 0:1],
                                in_=PS[5].ap[:, 0:1]),
               ins=(PS[5],), outs=(colb,))
            return colb

        mean_w1 = stream_reduce_scalar("w1T", 32, 1024, ALU.add,
                                       postmean=1.0 / (DFF * D))
        mean_w2 = stream_reduce_scalar("w2T", 8, 4096, ALU.add,
                                       postmean=1.0 / (DFF * D))
        s_w1c = col_of(mean_w1, recip=True)
        s_w2c = col_of(mean_w2, recip=True)

        def quant_stream(dramname, nblk, K, scol, emit_dst):
            AR.mark()
            wf3 = AR.alloc(4096, F32)
            qi = AR.alloc(4096, I32)
            qf = AR.alloc(4096, F32)
            qb = AR.alloc(4096, BF16)
            for blk in range(nblk):
                wslab(dramname, blk, K, wf3)
                ve(lambda e, K=K: e.tensor_scalar(
                    out=qf.ap[:, 0:K], in0=wf3.ap[:, 0:K],
                    scalar1=scol.ap[:, 0:1], scalar2=None, op0=ALU.mult),
                   ins=(wf3, scol), outs=(qf,))
                ve(lambda e, K=K: e.tensor_copy(out=qi.ap[:, 0:K],
                                                in_=qf.ap[:, 0:K]),
                   ins=(qf,), outs=(qi,))
                ve(lambda e, K=K: e.tensor_copy(out=qf.ap[:, 0:K],
                                                in_=qi.ap[:, 0:K]),
                   ins=(qi,), outs=(qf,))
                ve(lambda e, K=K: e.tensor_scalar(
                    out=qb.ap[:, 0:K], in0=qf.ap[:, 0:K], scalar1=-1.0,
                    scalar2=1.0, op0=ALU.max, op1=ALU.min),
                   ins=(qf,), outs=(qb,))
                emit_dst(blk, qb)
            AR.release()

        AR.mark()
        WQ1 = [AR.alloc(1024, BF16) for _ in range(NFT)]

        def w1_dst(blk, qb):
            ve(lambda e, blk=blk: e.tensor_copy(out=WQ1[blk].ap[:],
                                                in_=qb.ap[:, 0:1024]),
               ins=(qb,), outs=(WQ1[blk],))
        quant_stream("w1T", 32, 1024, s_w1c, w1_dst)

        QIN = [AR.alloc(T, BF16) for _ in range(NJ)]
        qin_h = CR.alloc(2 * NJ, BF16)
        srow_x, sirow_c, rs_x = RS[4], RS[2], RS[3]
        AR.mark()
        xmw = AR.alloc(NJ * TC, F32)
        xm = [Buf(xmw.ap[:, j * TC:(j + 1) * TC]) for j in range(NJ)]
        Mt = AR.alloc(TC, F32)
        xt2 = AR.alloc(TC, F32)
        for t in range(4):
            bcast(mu0, t * TC, TC, PS[6])
            for j in range(NJ):
                dma("sy", xt2.ap[:],
                    din["x0"][j * 128:(j + 1) * 128, t * TC:(t + 1) * TC],
                    outs=(xt2,))
                ve(lambda e, j=j: e.tensor_tensor(
                    out=xm[j].ap[:], in0=xt2.ap[:], in1=PS[6].ap[:, 0:TC],
                    op=ALU.subtract), ins=(xt2, PS[6]), outs=(xm[j],))
            ve(lambda e: e.tensor_reduce(
                out=Mt.ap[:],
                in_=xmw.ap[:, 0:NJ * TC].rearrange("p (j t) -> p t j",
                                                   j=NJ),
                axis=AX.X, op=ALU.max, apply_absolute_value=True),
               ins=tuple(xm), outs=(Mt,))
            for k in range(0, TC, 128):
                w = min(128, TC - k)
                gt = t * TC + k
                te(lambda e, k=k, w=w: e.transpose(
                    out=PS[5].ap[0:w, 0:128],
                    in_=Mt.ap[:, k:k + w],
                    identity=ident_f.ap[:]), ins=(Mt, ident_f),
                   outs=(PS[5],))
                ve(lambda e, w=w, pb=gt % 128, c=gt // 128:
                   e.tensor_reduce(
                    out=mxc.ap[pb:pb + w, c:c + 1],
                    in_=PS[5].ap[0:w, 0:128], axis=AX.X, op=ALU.max,
                    apply_absolute_value=True), ins=(PS[5],),
                   outs=(mxc,))
        AR.release()
        te(lambda e: e.transpose(out=PS[5].ap[0:nkT, 0:128],
                                 in_=mxc.ap[:, 0:nkT],
                                 identity=ident_f.ap[:]),
           ins=(mxc, ident_f), outs=(PS[5],))
        se(lambda e: e.copy(out=mxsb.ap[0:nkT, 0:128],
                            in_=PS[5].ap[0:nkT, 0:128]),
           ins=(PS[5],), outs=(mxsb,))
        dma("se", srow_x.ap[0:1, 0:T], mxsb.ap[0:nkT, 0:128],
            ins=(mxsb,), outs=(srow_x,))

        def finish_scale_rows(sro, r_row, siro, width, meanrow):
            ve(lambda e, w=width: e.tensor_tensor(
                out=sro.ap[0:1, 0:w], in0=sro.ap[0:1, 0:w],
                in1=r_row.ap[0:1, 0:w], op=ALU.mult),
               ins=(sro, r_row), outs=(sro,))
            ve(lambda e, w=width: e.tensor_scalar(
                out=sro.ap[0:1, 0:w], in0=sro.ap[0:1, 0:w], scalar1=1e-5,
                scalar2=None, op0=ALU.max), ins=(sro,), outs=(sro,))
            ve(lambda e, w=width: e.tensor_scalar(
                out=siro.ap[0:1, 0:w], in0=sro.ap[0:1, 0:w],
                scalar1=1.0 / 127.0, scalar2=None, op0=ALU.mult),
               ins=(sro,), outs=(siro,))
            ve(lambda e, w=width: e.tensor_scalar(
                out=siro.ap[0:1, 0:w], in0=siro.ap[0:1, 0:w],
                scalar1=meanrow.ap[0:1, 0:1], scalar2=None,
                op0=ALU.mult), ins=(siro, meanrow), outs=(siro,))
            ve(lambda e, w=width: e.reciprocal(
                out=sro.ap[0:1, 0:w], in_=sro.ap[0:1, 0:w]),
               ins=(sro,), outs=(sro,))
            ve(lambda e, w=width: e.tensor_scalar(
                out=sro.ap[0:1, 0:w], in0=sro.ap[0:1, 0:w],
                scalar1=127.0, scalar2=None, op0=ALU.mult),
               ins=(sro,), outs=(sro,))

        finish_scale_rows(srow_x, r0, sirow_c, T, mean_w1)
        ve(lambda e: e.tensor_tensor(out=rs_x.ap[0:1, 0:T],
                                     in0=r0.ap[0:1, 0:T],
                                     in1=srow_x.ap[0:1, 0:T],
                                     op=ALU.mult),
           ins=(r0, srow_x), outs=(rs_x,))
        AR.mark()
        tmpe = AR.alloc(TC, F32)
        qie = AR.alloc(TC, I32)
        for t in range(4):
            bcast(rs_x, t * TC, TC, PS[7])
            bcast(mu0, t * TC, TC, PS[6])
            for j in range(NJ):
                dma("sy", tmpe.ap[:],
                    din["x0"][j * 128:(j + 1) * 128, t * TC:(t + 1) * TC],
                    outs=(tmpe,))
                ve(lambda e: e.tensor_tensor(out=tmpe.ap[:], in0=tmpe.ap[:],
                                             in1=PS[6].ap[:, 0:TC],
                                             op=ALU.subtract),
                   ins=(tmpe, PS[6]), outs=(tmpe,))
                ve(lambda e: e.tensor_tensor(out=tmpe.ap[:], in0=tmpe.ap[:],
                                             in1=PS[7].ap[:, 0:TC],
                                             op=ALU.mult),
                   ins=(tmpe, PS[7]), outs=(tmpe,))
                ve(lambda e: e.tensor_copy(out=qie.ap[:], in_=tmpe.ap[:]),
                   ins=(tmpe,), outs=(qie,))
                ve(lambda e, j=j, t=t: e.tensor_copy(
                    out=QIN[j].ap[:, t * TC:(t + 1) * TC], in_=qie.ap[:]),
                   ins=(qie,), outs=(QIN[j],))
        AR.release()
        dbg_dump("d_qin", QIN)
        hsr = srow(2)
        hsi = srow(2)
        hrs = srow(2)
        xmh = CR.alloc(2 * NJ, F32)
        Mh = CR.alloc(4, F32)
        hcol2 = CR.alloc(2, F32)
        bcast(mu0h, 0, 2, PS[6])
        for j in range(NJ):
            ve(lambda e, j=j: e.tensor_tensor(
                out=xmh.ap[:, 2 * j:2 * j + 2],
                in0=x0h_t.ap[:, 2 * j:2 * j + 2], in1=PS[6].ap[:, 0:2],
                op=ALU.subtract), ins=(x0h_t, PS[6]), outs=(xmh,))
        ve(lambda e: e.tensor_reduce(
            out=Mh.ap[:, 0:2],
            in_=xmh.ap[:, 0:2 * NJ].rearrange("p (j i) -> p i j", j=NJ),
            axis=AX.X, op=ALU.max, apply_absolute_value=True),
           ins=(xmh,), outs=(Mh,))
        te(lambda e: e.transpose(out=PS[5].ap[0:2, 0:128],
                                 in_=Mh.ap[:, 0:2],
                                 identity=ident_f.ap[:]),
           ins=(Mh, ident_f), outs=(PS[5],))
        ve(lambda e: e.tensor_reduce(out=hcol2.ap[0:2, 0:1],
                                     in_=PS[5].ap[0:2, 0:128], axis=AX.X,
                                     op=ALU.max,
                                     apply_absolute_value=True),
           ins=(PS[5],), outs=(hcol2,))
        te(lambda e: e.transpose(out=PS[5].ap[0:1, 0:2],
                                 in_=hcol2.ap[0:2, 0:1],
                                 identity=ident_f.ap[0:2, 0:2]),
           ins=(hcol2, ident_f), outs=(PS[5],))
        se(lambda e: e.copy(out=hsr.ap[0:1, 0:2],
                            in_=PS[5].ap[0:1, 0:2]),
           ins=(PS[5],), outs=(hsr,))
        finish_scale_rows(hsr, r0h, hsi, 2, mean_w1)
        ve(lambda e: e.tensor_tensor(out=hrs.ap[0:1, 0:2],
                                     in0=r0h.ap[0:1, 0:2],
                                     in1=hsr.ap[0:1, 0:2], op=ALU.mult),
           ins=(r0h, hsr), outs=(hrs,))
        bcast(hrs, 0, 2, PS[7])
        AR.mark()
        tmph = AR.alloc(2, F32)
        qih = AR.alloc(2, I32)
        for j in range(NJ):
            ve(lambda e, j=j: e.tensor_tensor(
                out=tmph.ap[:, 0:2], in0=xmh.ap[:, 2 * j:2 * j + 2],
                in1=PS[7].ap[:, 0:2], op=ALU.mult),
               ins=(xmh, PS[7]), outs=(tmph,))
            ve(lambda e: e.tensor_copy(out=qih.ap[:, 0:2],
                                       in_=tmph.ap[:, 0:2]),
               ins=(tmph,), outs=(qih,))
            ve(lambda e, j=j: e.tensor_copy(
                out=qin_h.ap[:, 2 * j:2 * j + 2], in_=qih.ap[:, 0:2]),
               ins=(qih,), outs=(qin_h,))
        AR.release()
        AR.mark()
        hb = AR.alloc(T, F32)
        hbh = AR.alloc(2, F32)
        acc = AR.alloc(T, F32)
        hcb = AR.alloc(T, BF16)
        sxc = AR.alloc(T, F32)
        hsib = AR.alloc(2, F32)
        for t in range(4):
            bcast(sirow_c, t * TC, TC, PS[6])
            se(lambda e, t=t: e.copy(out=sxc.ap[:, t * TC:(t + 1) * TC],
                                     in_=PS[6].ap[:, 0:TC]),
               ins=(PS[6],), outs=(sxc,))
        bcast(hsi, 0, 2, PS[6])
        se(lambda e: e.copy(out=hsib.ap[:, 0:2], in_=PS[6].ap[:, 0:2]),
           ins=(PS[6],), outs=(hsib,))
        for ot in range(NFT):
            for t in range(4):
                mm([WQ1[ot].ap[:, kc * 128:(kc + 1) * 128]
                    for kc in range(8)],
                   [QIN[kc].ap[:, t * TC:(t + 1) * TC]
                    for kc in range(8)],
                   PS[t % 4].ap[:, 0:TC], ins=[WQ1[ot]] + QIN,
                   outs=(PS[t % 4],))
                ve(lambda e, t=t, ps=PS[t % 4]: e.tensor_tensor(
                    out=hb.ap[:, t * TC:(t + 1) * TC],
                    in0=ps.ap[:, 0:TC], in1=sxc.ap[:, t * TC:(t + 1) * TC],
                    op=ALU.mult), ins=(PS[t % 4], sxc), outs=(hb,))
            mm([WQ1[ot].ap[:, kc * 128:(kc + 1) * 128]
                for kc in range(8)],
               [qin_h.ap[:, 2 * kc:2 * kc + 2] for kc in range(8)],
               PS[4].ap[:, 0:2], ins=[WQ1[ot], qin_h], outs=(PS[4],))
            ve(lambda e: e.tensor_tensor(out=hbh.ap[:, 0:2],
                                         in0=PS[4].ap[:, 0:2],
                                         in1=hsib.ap[:, 0:2],
                                         op=ALU.mult),
               ins=(PS[4], hsib), outs=(hbh,))
            ve(lambda e, ot=ot: e.tensor_scalar(
                out=acc.ap[:], in0=hb.ap[:],
                scalar1=cols_t["cw1"].ap[:, ot:ot + 1], scalar2=None,
                op0=ALU.mult), ins=(hb, cols_t["cw1"]), outs=(acc,))
            ve(lambda e, ot=ot: e.scalar_tensor_tensor(
                out=acc.ap[:, 1:T], in0=hb.ap[:, 0:T - 1],
                scalar=cols_t["cw0"].ap[:, ot:ot + 1],
                in1=acc.ap[:, 1:T], op0=ALU.mult, op1=ALU.add),
               ins=(hb, cols_t["cw0"], acc), outs=(acc,))
            ve(lambda e, ot=ot: e.scalar_tensor_tensor(
                out=acc.ap[:, 0:1], in0=hbh.ap[:, 0:1],
                scalar=cols_t["cw0"].ap[:, ot:ot + 1],
                in1=acc.ap[:, 0:1], op0=ALU.mult, op1=ALU.add),
               ins=(hbh, cols_t["cw0"], acc), outs=(acc,))
            ve(lambda e, ot=ot: e.scalar_tensor_tensor(
                out=acc.ap[:, 0:T - 1], in0=hb.ap[:, 1:T],
                scalar=cols_t["cw2"].ap[:, ot:ot + 1],
                in1=acc.ap[:, 0:T - 1], op0=ALU.mult, op1=ALU.add),
               ins=(hb, cols_t["cw2"], acc), outs=(acc,))
            ve(lambda e, ot=ot: e.scalar_tensor_tensor(
                out=acc.ap[:, T - 1:T], in0=hbh.ap[:, 1:2],
                scalar=cols_t["cw2"].ap[:, ot:ot + 1],
                in1=acc.ap[:, T - 1:T], op0=ALU.mult, op1=ALU.add),
               ins=(hbh, cols_t["cw2"], acc), outs=(acc,))
            se(lambda e, ot=ot: e.activation(
                out=hcb.ap[:], in_=acc.ap[:], func=AF.Silu,
                bias=cols_t["cb"].ap[:, ot:ot + 1]),
               ins=(acc, cols_t["cb"]), outs=(hcb,))
            dma("se", sc_hc.ap[ot], hcb.ap[:], ins=(hcb,), outs=(sc_hc,))
            if DBG and "d_h" in dbg_d:
                dma("se", dbg_d["d_h"][ot * 128:(ot + 1) * 128, :],
                    hb.ap[:], ins=(hb,))
            if DBG and "d_hc" in dbg_d:
                AR.mark()
                tq = AR.alloc(T, F32)
                ve(lambda e, tq=tq, hcb=hcb: e.tensor_copy(out=tq.ap[:], in_=hcb.ap[:]),
                   ins=(hcb,), outs=(tq,))
                dma("se", dbg_d["d_hc"][ot * 128:(ot + 1) * 128, :],
                    tq.ap[:], ins=(tq,))
                AR.release()
        AR.release()
        AR.release()

        srow2, sirow2 = RS[4], RS[2]
        AR.mark()
        AR.mark()
        hcw = AR.alloc(NFT * TC, BF16)
        hcs = [Buf(hcw.ap[:, o * TC:(o + 1) * TC]) for o in range(NFT)]
        Mt2 = AR.alloc(TC, F32)
        for t in range(4):
            for ot in range(NFT):
                dma("sy", hcs[ot].ap[:],
                    sc_hc.ap[ot, :, t * TC:(t + 1) * TC],
                    ins=(sc_hc,), outs=(hcs[ot],))
            ve(lambda e: e.tensor_reduce(
                out=Mt2.ap[:],
                in_=hcw.ap[:, 0:NFT * TC].rearrange("p (o t) -> p t o",
                                                    o=NFT),
                axis=AX.X, op=ALU.max, apply_absolute_value=True),
               ins=tuple(hcs), outs=(Mt2,))
            for k in range(0, TC, 128):
                w = min(128, TC - k)
                gt = t * TC + k
                te(lambda e, k=k, w=w: e.transpose(
                    out=PS[5].ap[0:w, 0:128],
                    in_=Mt2.ap[:, k:k + w],
                    identity=ident_f.ap[:]), ins=(Mt2, ident_f),
                   outs=(PS[5],))
                ve(lambda e, w=w, pb=gt % 128, c=gt // 128:
                   e.tensor_reduce(
                    out=mxc.ap[pb:pb + w, c:c + 1],
                    in_=PS[5].ap[0:w, 0:128], axis=AX.X, op=ALU.max,
                    apply_absolute_value=True), ins=(PS[5],),
                   outs=(mxc,))
        AR.release()
        te(lambda e: e.transpose(out=PS[5].ap[0:nkT, 0:128],
                                 in_=mxc.ap[:, 0:nkT],
                                 identity=ident_f.ap[:]),
           ins=(mxc, ident_f), outs=(PS[5],))
        se(lambda e: e.copy(out=mxsb.ap[0:nkT, 0:128],
                            in_=PS[5].ap[0:nkT, 0:128]),
           ins=(PS[5],), outs=(mxsb,))
        dma("se", srow2.ap[0:1, 0:T], mxsb.ap[0:nkT, 0:128],
            ins=(mxsb,), outs=(srow2,))
        ve(lambda e: e.tensor_scalar(out=srow2.ap[0:1, 0:T],
                                     in0=srow2.ap[0:1, 0:T],
                                     scalar1=1e-5, scalar2=None,
                                     op0=ALU.max),
           ins=(srow2,), outs=(srow2,))
        ve(lambda e: e.tensor_scalar(out=sirow2.ap[0:1, 0:T],
                                     in0=srow2.ap[0:1, 0:T],
                                     scalar1=1.0 / 127.0, scalar2=None,
                                     op0=ALU.mult), ins=(srow2,),
           outs=(sirow2,))
        ve(lambda e: e.tensor_scalar(out=sirow2.ap[0:1, 0:T],
                                     in0=sirow2.ap[0:1, 0:T],
                                     scalar1=mean_w2.ap[0:1, 0:1],
                                     scalar2=None, op0=ALU.mult),
           ins=(sirow2, mean_w2), outs=(sirow2,))
        ve(lambda e: e.reciprocal(out=srow2.ap[0:1, 0:T],
                                  in_=srow2.ap[0:1, 0:T]),
           ins=(srow2,), outs=(srow2,))
        ve(lambda e: e.tensor_scalar(out=srow2.ap[0:1, 0:T],
                                     in0=srow2.ap[0:1, 0:T],
                                     scalar1=127.0, scalar2=None,
                                     op0=ALU.mult),
           ins=(srow2,), outs=(srow2,))
        WQ2 = [AR.alloc(1024, BF16) for _ in range(NFT)]

        def w2_dst(blk, qb):
            for kc in range(NFT):
                ve(lambda e, kc=kc, blk=blk: e.tensor_copy(
                    out=WQ2[kc].ap[:, blk * 128:(blk + 1) * 128],
                    in_=qb.ap[:, kc * 128:(kc + 1) * 128]),
                   ins=(qb,), outs=(WQ2[kc],))
        quant_stream("w2T", 8, 4096, s_w2c, w2_dst)
        AR.mark()
        hq = [AR.alloc(TC, BF16) for _ in range(NFT)]
        s0t = AR.alloc(TC, F32)
        hct = AR.alloc(TC, BF16)
        tmp2 = AR.alloc(TC, F32)
        qi2 = AR.alloc(TC, I32)
        xt3 = AR.alloc(TC, F32)
        g0p = AR.alloc(TC, F32)
        sir2sb = AR.alloc(TC, F32)
        for t in range(4):
            bcast(srow2, t * TC, TC, PS[6])
            for ot in range(NFT):
                dma("sy", hct.ap[:],
                    sc_hc.ap[ot, :, t * TC:(t + 1) * TC],
                    ins=(sc_hc,), outs=(hct,))
                ve(lambda e: e.tensor_tensor(out=tmp2.ap[:],
                                             in0=hct.ap[:],
                                             in1=PS[6].ap[:, 0:TC],
                                             op=ALU.mult),
                   ins=(hct, PS[6]), outs=(tmp2,))
                ve(lambda e: e.tensor_copy(out=qi2.ap[:],
                                           in_=tmp2.ap[:]),
                   ins=(tmp2,), outs=(qi2,))
                ve(lambda e, ot=ot: e.tensor_copy(out=hq[ot].ap[:],
                                                  in_=qi2.ap[:]),
                   ins=(qi2,), outs=(hq[ot],))
            bcast(sirow2, t * TC, TC, PS[6])
            se(lambda e: e.copy(out=sir2sb.ap[:, 0:TC],
                                in_=PS[6].ap[:, 0:TC]),
               ins=(PS[6],), outs=(sir2sb,))
            for j in range(NJ):
                mm([WQ2[kc].ap[:, j * 128:(j + 1) * 128]
                    for kc in range(NFT)],
                   [hq[kc].ap[:] for kc in range(NFT)],
                   PS[j % 4].ap[:, 0:TC], ins=WQ2 + hq,
                   outs=(PS[j % 4],))
                dma("sy", s0t.ap[:], sc_s0.ap[j, :, t * TC:(t + 1) * TC],
                    ins=(sc_s0,), outs=(s0t,))
                dma("sy", xt3.ap[:],
                    din["x0"][j * 128:(j + 1) * 128, t * TC:(t + 1) * TC],
                    outs=(xt3,))
                ve(lambda e, ps=PS[j % 4]: e.tensor_tensor(
                    out=g0p.ap[:], in0=ps.ap[:, 0:TC],
                    in1=sir2sb.ap[:, 0:TC], op=ALU.mult),
                   ins=(PS[j % 4], sir2sb), outs=(g0p,))
                se(lambda e: e.activation(out=g0p.ap[:], in_=g0p.ap[:],
                                          func=AF.Silu),
                   ins=(g0p,), outs=(g0p,))
                if DBG and "d_g0" in dbg_d:
                    dma("se", dbg_d["d_g0"][j * 128:(j + 1) * 128,
                                            t * TC:(t + 1) * TC],
                        g0p.ap[:], ins=(g0p,))
                ve(lambda e: e.tensor_tensor(out=g0p.ap[:],
                                             in0=g0p.ap[:],
                                             in1=s0t.ap[:],
                                             op=ALU.mult),
                   ins=(g0p, s0t), outs=(g0p,))
                ve(lambda e: e.tensor_tensor(out=xt3.ap[:],
                                             in0=xt3.ap[:],
                                             in1=g0p.ap[:],
                                             op=ALU.add),
                   ins=(g0p, xt3), outs=(xt3,))
                dma("se", sc_x1.ap[j, :, t * TC:(t + 1) * TC], xt3.ap[:],
                    ins=(xt3,), outs=(sc_x1,))
                if t == 3:
                    dma("se", agh_in.ap[j, :, 0:3], xt3.ap[:, TC - 3:TC],
                        ins=(xt3,), outs=(agh_in,))
                if DBG and "d_x1" in dbg_d:
                    dma("se", dbg_d["d_x1"][j * 128:(j + 1) * 128,
                                            t * TC:(t + 1) * TC],
                        xt3.ap[:], ins=(xt3,))
        AR.release()
        AR.release()
        cc("AllGather", agh_in, agh_out)

        AR.mark()
        G1 = [AR.alloc(T, BF16) for _ in range(NJ)]
        AR.mark()
        XN1 = [AR.alloc(T, BF16) for _ in range(NJ)]
        mu3, r3 = RS[3], RS[4]
        AR.mark()
        xf2 = AR.alloc(512, F32)

        def fill_x1(j, lo, w, xb):
            dma("sy", xf2.ap[:, 0:w], sc_x1.ap[j, :, lo:lo + w],
                ins=(sc_x1,), outs=(xf2,))
            ve(lambda e, w=w: e.tensor_copy(out=xb.ap[:, 0:w],
                                            in_=xf2.ap[:, 0:w]),
               ins=(xf2,), outs=(xb,))
        ln_stats(fill_x1, NJ, T, mu3, r3)
        tmp4 = AR.alloc(TC, F32)
        for t in range(4):
            bcast(mu3, t * TC, TC, PS[6])
            bcast(r3, t * TC, TC, PS[7])
            for j in range(NJ):
                dma("sy", tmp4.ap[:], sc_x1.ap[j, :, t * TC:(t + 1) * TC],
                    ins=(sc_x1,), outs=(tmp4,))
                ve(lambda e: e.tensor_tensor(out=tmp4.ap[:],
                                             in0=tmp4.ap[:],
                                             in1=PS[6].ap[:, 0:TC],
                                             op=ALU.subtract),
                   ins=(tmp4, PS[6]), outs=(tmp4,))
                ve(lambda e, j=j, t=t: e.tensor_tensor(
                    out=XN1[j].ap[:, t * TC:(t + 1) * TC],
                    in0=tmp4.ap[:], in1=PS[7].ap[:, 0:TC], op=ALU.mult),
                   ins=(tmp4, PS[7]), outs=(XN1[j],))
        AR.release()
        x1h = CR.alloc(3 * NJ, F32)
        xn1h = CR.alloc(3 * NJ, BF16)
        muh = srow(3)
        rh = srow(3)
        AR.mark()
        slots = AR.alloc(12, F32)
        acch = AR.alloc(3, F32)
        for j in range(NJ):
            for r in range(4):
                dma("sy", slots.ap[:, 3 * r:3 * r + 3], agh_out.ap[r, j],
                    ins=(agh_out,), outs=(slots,))
            ve(lambda e: e.tensor_scalar(
                out=acch.ap[:, 0:3], in0=slots.ap[:, 0:3],
                scalar1=halo_m.ap[:, 0:1], scalar2=None, op0=ALU.mult),
               ins=(slots, halo_m), outs=(acch,))
            for r in range(1, 4):
                ve(lambda e, r=r: e.scalar_tensor_tensor(
                    out=acch.ap[:, 0:3],
                    in0=slots.ap[:, 3 * r:3 * r + 3],
                    scalar=halo_m.ap[:, r:r + 1], in1=acch.ap[:, 0:3],
                    op0=ALU.mult, op1=ALU.add),
                   ins=(slots, halo_m, acch), outs=(acch,))
            ve(lambda e, j=j: e.tensor_copy(
                out=x1h.ap[:, 3 * j:3 * j + 3], in_=acch.ap[:, 0:3]),
               ins=(acch,), outs=(x1h,))
        ln_stats(fill_from_cr(x1h, percol=3), NJ, 3, muh, rh)
        bcast(muh, 0, 3, PS[6])
        bcast(rh, 0, 3, PS[7])
        tmp3 = AR.alloc(3, F32)
        for j in range(NJ):
            ve(lambda e, j=j: e.tensor_tensor(
                out=tmp3.ap[:, 0:3], in0=x1h.ap[:, 3 * j:3 * j + 3],
                in1=PS[6].ap[:, 0:3], op=ALU.subtract),
               ins=(x1h, PS[6]), outs=(tmp3,))
            ve(lambda e, j=j: e.tensor_tensor(
                out=xn1h.ap[:, 3 * j:3 * j + 3], in0=tmp3.ap[:, 0:3],
                in1=PS[7].ap[:, 0:3], op=ALU.mult),
               ins=(tmp3, PS[7]), outs=(xn1h,))
        AR.release()

        AR.mark()
        UCB = [AR.alloc(T, BF16) for _ in range(NOT)]
        AR.mark()
        sla = AR.alloc(1024, BF16)
        uf = AR.alloc(T, F32)
        uh = AR.alloc(3, F32)
        accu = AR.alloc(T, F32)
        zbb = AR.alloc(T, BF16)
        for ot in range(NFT):
            wslab("W_in", ot, 1024, sla)
            if ot < NOT:
                for t in range(4):
                    mm([sla.ap[:, kc * 128:(kc + 1) * 128]
                        for kc in range(8)],
                       [XN1[kc].ap[:, t * TC:(t + 1) * TC]
                        for kc in range(8)],
                       PS[t % 4].ap[:, 0:TC], ins=[sla] + XN1,
                       outs=(PS[t % 4],))
                    se(lambda e, t=t, ps=PS[t % 4]: e.copy(
                        out=uf.ap[:, t * TC:(t + 1) * TC],
                        in_=ps.ap[:, 0:TC]), ins=(PS[t % 4],), outs=(uf,))
                mm([sla.ap[:, kc * 128:(kc + 1) * 128]
                    for kc in range(8)],
                   [xn1h.ap[:, 3 * kc:3 * kc + 3] for kc in range(8)],
                   PS[4].ap[:, 0:3], ins=[sla, xn1h], outs=(PS[4],))
                se(lambda e: e.copy(out=uh.ap[:, 0:3],
                                    in_=PS[4].ap[:, 0:3]),
                   ins=(PS[4],), outs=(uh,))
                ve(lambda e, ot=ot: e.tensor_scalar(
                    out=accu.ap[:], in0=uf.ap[:],
                    scalar1=cols_t["c1w3"].ap[:, ot:ot + 1],
                    scalar2=None, op0=ALU.mult),
                   ins=(uf, cols_t["c1w3"]), outs=(accu,))
                for sh, cw in [(3, "c1w0"), (2, "c1w1"), (1, "c1w2")]:
                    ve(lambda e, ot=ot, sh=sh, cw=cw:
                       e.scalar_tensor_tensor(
                           out=accu.ap[:, sh:T], in0=uf.ap[:, 0:T - sh],
                           scalar=cols_t[cw].ap[:, ot:ot + 1],
                           in1=accu.ap[:, sh:T], op0=ALU.mult,
                           op1=ALU.add),
                       ins=(uf, cols_t[cw], accu), outs=(accu,))
                    ve(lambda e, ot=ot, sh=sh, cw=cw:
                       e.scalar_tensor_tensor(
                           out=accu.ap[:, 0:sh],
                           in0=uh.ap[:, 3 - sh:3],
                           scalar=cols_t[cw].ap[:, ot:ot + 1],
                           in1=accu.ap[:, 0:sh], op0=ALU.mult,
                           op1=ALU.add),
                       ins=(uh, cols_t[cw], accu), outs=(accu,))
                se(lambda e, ot=ot: e.activation(
                    out=UCB[ot].ap[:], in_=accu.ap[:], func=AF.Silu,
                    bias=cols_t["c1b"].ap[:, ot:ot + 1]),
                   ins=(accu, cols_t["c1b"]), outs=(UCB[ot],))
            else:
                oz = ot - NOT
                for t in range(4):
                    mm([sla.ap[:, kc * 128:(kc + 1) * 128]
                        for kc in range(8)],
                       [XN1[kc].ap[:, t * TC:(t + 1) * TC]
                        for kc in range(8)],
                       PS[t % 4].ap[:, 0:TC], ins=[sla] + XN1,
                       outs=(PS[t % 4],))
                    se(lambda e, t=t, ps=PS[t % 4]: e.copy(
                        out=zbb.ap[:, t * TC:(t + 1) * TC],
                        in_=ps.ap[:, 0:TC]), ins=(PS[t % 4],), outs=(zbb,))
                dma("se", sc_z.ap[oz], zbb.ap[:], ins=(zbb,),
                    outs=(sc_z,))
        AR.release()
        dbg_dump("d_uc", UCB)

        AR.mark()
        dtrb = AR.alloc(T, BF16)
        negbc = AR.alloc(T, BF16)
        AR.mark()
        wx = AR.alloc(16 * 160, BF16)
        for kc in range(16):
            dma("sy", wx.ap[:, kc * 160:(kc + 1) * 160],
                din["W_x"][kc * 128:(kc + 1) * 128, :], outs=(wx,))
        for t in range(4):
            mm([wx.ap[:, kc * 160:kc * 160 + 128] for kc in range(16)],
               [UCB[kc].ap[:, t * TC:(t + 1) * TC] for kc in range(16)],
               PS[0].ap[:, 0:TC], ins=[wx] + UCB, outs=(PS[0],))
            se(lambda e, t=t: e.copy(out=dtrb.ap[:, t * TC:(t + 1) * TC],
                                     in_=PS[0].ap[:, 0:TC]),
               ins=(PS[0],), outs=(dtrb,))
            mm([wx.ap[:, kc * 160 + 128:(kc + 1) * 160]
                for kc in range(16)],
               [UCB[kc].ap[:, t * TC:(t + 1) * TC] for kc in range(16)],
               PS[1].ap[0:32, 0:TC], ins=[wx] + UCB, outs=(PS[1],))
            ve(lambda e, t=t: e.tensor_scalar(
                out=negbc.ap[0:32, t * TC:(t + 1) * TC],
                in0=PS[1].ap[0:32, 0:TC],
                scalar1=sgnbc.ap[0:32, 0:1], scalar2=None,
                op0=ALU.mult), ins=(PS[1], sgnbc), outs=(negbc,))
        dma("se", agbc_in.ap[:], negbc.ap[0:32, :], ins=(negbc,),
            outs=(agbc_in,))
        cc("AllGather", agbc_in, agbc_out, rg=RG8)
        if DBG and "d_bc" in dbg_d:
            AR.mark()
            tq = AR.alloc(T, F32)
            ve(lambda e, tq=tq: e.tensor_copy(out=tq.ap[0:32, 0:T],
                                       in_=negbc.ap[0:32, :]),
               ins=(negbc,), outs=(tq,))
            dma("se", dbg_d["d_bc"], tq.ap[0:32, 0:T], ins=(tq,))
            AR.release()

        AR.release()
        wdt = AR.alloc(DI, BF16)
        dma("sy",
            wdt.ap[:, 0:DI].rearrange("k (nb c) -> k nb c", nb=16),
            din["W_dt"].rearrange("nb k c -> k nb c"), outs=(wdt,))
        AR.mark()
        dtf = AR.alloc(T, BF16)
        axx = AR.alloc(TC, F32)
        exx = AR.alloc(TC, F32)
        rxx = AR.alloc(TC, F32)
        for ot in range(NOT):
            for t in range(4):
                mm([wdt.ap[:, ot * 128:(ot + 1) * 128]],
                   [dtrb.ap[:, t * TC:(t + 1) * TC]],
                   PS[t % 2].ap[:, 0:TC], ins=(wdt, dtrb),
                   outs=(PS[t % 2],))
                se(lambda e, ot=ot, ps=PS[t % 2]: e.activation(
                    out=axx.ap[:], in_=ps.ap[:, 0:TC], func=AF.Abs,
                    bias=cols_t["bdt"].ap[:, ot:ot + 1]),
                   ins=(PS[t % 2], cols_t["bdt"]), outs=(axx,))
                se(lambda e, ot=ot, ps=PS[t % 2]: e.activation(
                    out=rxx.ap[:], in_=ps.ap[:, 0:TC], func=AF.Relu,
                    bias=cols_t["bdt"].ap[:, ot:ot + 1]),
                   ins=(PS[t % 2], cols_t["bdt"]), outs=(rxx,))
                se(lambda e: e.activation(out=exx.ap[:], in_=axx.ap[:],
                                          func=AF.Exp, scale=-1.0),
                   ins=(axx,), outs=(exx,))
                ve(lambda e: e.tensor_scalar(out=exx.ap[:],
                                             in0=exx.ap[:],
                                             scalar1=1.0, scalar2=None,
                                             op0=ALU.add),
                   ins=(exx,), outs=(exx,))
                se(lambda e: e.activation(out=exx.ap[:], in_=exx.ap[:],
                                          func=AF.Ln), ins=(exx,),
                   outs=(exx,))
                ve(lambda e, t=t: e.tensor_tensor(
                    out=dtf.ap[:, t * TC:(t + 1) * TC], in0=exx.ap[:],
                    in1=rxx.ap[:], op=ALU.add), ins=(exx, rxx),
                   outs=(dtf,))
            sl_, jj = ot // 2, ot % 2
            dma("se", a1_in.ap[sl_, 0, jj], dtf.ap[:], ins=(dtf,),
                outs=(a1_in,))
            dma("se", a1_in.ap[sl_, 1, jj], UCB[ot].ap[:],
                ins=(UCB[ot],), outs=(a1_in,))
            if DBG and "d_dt" in dbg_d:
                AR.mark()
                tq = AR.alloc(T, F32)
                ve(lambda e, tq=tq, dtf=dtf: e.tensor_copy(out=tq.ap[:], in_=dtf.ap[:]),
                   ins=(dtf,), outs=(tq,))
                dma("se", dbg_d["d_dt"][ot * 128:(ot + 1) * 128, :],
                    tq.ap[:], ins=(tq,))
                AR.release()
        AR.release()
        AR.release()
        AR.release()
        cc("AllToAll", a1_in, a1_out, rg=RG8)

        kmax = stream_reduce_scalar("Kf", NJ, 1024, ALU.max)
        kscol = col_of(kmax, scale=1.0 / 7.0)
        ksrcol = col_of(kmax, recip=True)
        ve(lambda e: e.tensor_scalar(out=ksrcol.ap[:, 0:1],
                                     in0=ksrcol.ap[:, 0:1], scalar1=7.0,
                                     scalar2=None, op0=ALU.mult),
           ins=(ksrcol,), outs=(ksrcol,))
        AR.mark()
        KQ = [AR.alloc(1024, BF16) for _ in range(NJ)]
        wf3 = AR.alloc(1024, F32)
        qi3 = AR.alloc(1024, I32)
        for blk in range(NJ):
            wslab("Kf", blk, 1024, wf3)
            ve(lambda e: e.tensor_scalar(out=wf3.ap[:, 0:1024],
                                         in0=wf3.ap[:, 0:1024],
                                         scalar1=ksrcol.ap[:, 0:1],
                                         scalar2=None, op0=ALU.mult),
               ins=(wf3, ksrcol), outs=(wf3,))
            ve(lambda e: e.tensor_copy(out=qi3.ap[:],
                                       in_=wf3.ap[:, 0:1024]),
               ins=(wf3,), outs=(qi3,))
            ve(lambda e, blk=blk: e.tensor_copy(out=KQ[blk].ap[:],
                                                in_=qi3.ap[:]),
               ins=(qi3,), outs=(KQ[blk],))
        alo = AR.alloc(8 * 16, BF16)
        blo = AR.alloc(D, BF16)
        blor = AR.alloc(D, BF16)
        for kc in range(8):
            dma("sy", alo.ap[:, kc * 16:(kc + 1) * 16],
                din["a_lora"][kc * 128:(kc + 1) * 128, :], outs=(alo,))
        dma("sy", blor.ap[0:16, :], din["b_lora"], outs=(blor,))
        ve(lambda e: e.tensor_scalar(out=blo.ap[0:16, :],
                                     in0=blor.ap[0:16, :],
                                     scalar1=ksrcol.ap[0:16, 0:1],
                                     scalar2=None, op0=ALU.mult),
           ins=(blor, ksrcol), outs=(blo,))
        r1b = AR.alloc(TC, BF16)
        for t in range(4):
            mm([alo.ap[:, kc * 16:(kc + 1) * 16] for kc in range(8)],
               [XN1[kc].ap[:, t * TC:(t + 1) * TC] for kc in range(8)],
               PS[4].ap[0:16, 0:TC], ins=[alo] + XN1, outs=(PS[4],))
            ve(lambda e: e.tensor_copy(out=r1b.ap[0:16, :],
                                       in_=PS[4].ap[0:16, 0:TC]),
               ins=(PS[4],), outs=(r1b,))
            for j in range(NJ):
                mm([KQ[j].ap[:, kc * 128:(kc + 1) * 128]
                    for kc in range(8)],
                   [XN1[kc].ap[:, t * TC:(t + 1) * TC]
                    for kc in range(8)],
                   PS[j % 4].ap[:, 0:TC], ins=KQ + XN1,
                   outs=(PS[j % 4],), final=False)
                mm([blo.ap[0:16, j * 128:(j + 1) * 128]],
                   [r1b.ap[0:16, :]],
                   PS[j % 4].ap[:, 0:TC], ins=(blo, r1b),
                   outs=(PS[j % 4],), accum=True)
                se(lambda e, j=j, t=t, ps=PS[j % 4]: e.activation(
                    out=G1[j].ap[:, t * TC:(t + 1) * TC],
                    in_=ps.ap[:, 0:TC], func=AF.Silu,
                    scale=kscol.ap[:, 0:1],
                    bias=cols_t["bias1f"].ap[:, j:j + 1]),
                   ins=(PS[j % 4], kscol, cols_t["bias1f"]), outs=(G1[j],))
        AR.release()
        dbg_dump("d_g1", G1)
        AR.release()

        AR.mark()
        negbcf = AR.alloc(L, BF16)
        CWS = min(1024, T)
        NCH = L // CWS
        W5 = min(512, CWS)
        N5 = CWS // W5
        stcols = AR.alloc(NST, F32)
        dts_c = AR.alloc(CWS, BF16)
        ucs_c = AR.alloc(CWS, BF16)
        dtu_c = AR.alloc(CWS, F32)
        dA_c = AR.alloc(CWS, F32)
        dBu_c = AR.alloc(CWS, F32)
        h_c = AR.alloc(CWS, F32)
        y_c = AR.alloc(L, F32)
        yt_c = AR.alloc(CWS, F32)
        ybf_c = AR.alloc(L, BF16)
        for b_ in range(2):
            for r in range(4):
                dma("sy", negbcf.ap[0:32, r * T:(r + 1) * T],
                    agbc_out.ap[4 * b_ + r], ins=(agbc_out,),
                    outs=(negbcf,))
            for oidx in range(2):
                ve(lambda e: e.memset(stcols.ap[:], 0.0), outs=(stcols,))
                for ch in range(NCH):
                    r_ = 4 * b_ + (ch * CWS) // T
                    ro = (ch * CWS) % T
                    dma("sy", dts_c.ap[:],
                        a1_out.ap[r_, 0, oidx][:, ro:ro + CWS],
                        ins=(a1_out,), outs=(dts_c,))
                    dma("sy", ucs_c.ap[:],
                        a1_out.ap[r_, 1, oidx][:, ro:ro + CWS],
                        ins=(a1_out,), outs=(ucs_c,))
                    ve(lambda e: e.tensor_tensor(out=dtu_c.ap[:],
                                                 in0=dts_c.ap[:],
                                                 in1=ucs_c.ap[:],
                                                 op=ALU.mult),
                       ins=(dts_c, ucs_c), outs=(dtu_c,))
                    for n in range(NST):
                        se(lambda e, oidx=oidx, n=n: e.activation(
                            out=dA_c.ap[:], in_=dts_c.ap[:], func=AF.Exp,
                            scale=negA.ap[:, oidx * NST + n:
                                          oidx * NST + n + 1]),
                           ins=(dts_c, negA), outs=(dA_c,))
                        for c5 in range(N5):
                            lo = (ch * CWS) % L + c5 * W5
                            mm([sel32.ap[0:32, n * 128:(n + 1) * 128]],
                               [negbcf.ap[0:32, ch * CWS + c5 * W5:
                                          ch * CWS + (c5 + 1) * W5]],
                               PS[c5 % 2].ap[:, 0:W5],
                               ins=(sel32, negbcf), outs=(PS[c5 % 2],))
                            ve(lambda e, c5=c5, ps=PS[c5 % 2]:
                               e.tensor_tensor(
                                   out=dBu_c.ap[:, c5 * W5:(c5 + 1) * W5],
                                   in0=dtu_c.ap[:, c5 * W5:(c5 + 1) * W5],
                                   in1=ps.ap[:, 0:W5], op=ALU.mult),
                               ins=(dtu_c, PS[c5 % 2]), outs=(dBu_c,))
                        ve(lambda e, n=n: e.tensor_tensor_scan(
                            out=h_c.ap[:], data0=dA_c.ap[:],
                            data1=dBu_c.ap[:],
                            initial=stcols.ap[:, n:n + 1], op0=ALU.mult,
                            op1=ALU.add),
                           ins=(dA_c, dBu_c, stcols), outs=(h_c,))
                        ve(lambda e, n=n: e.tensor_copy(
                            out=stcols.ap[:, n:n + 1],
                            in_=h_c.ap[:, CWS - 1:CWS]),
                           ins=(h_c,), outs=(stcols,))
                        for c5 in range(N5):
                            mm([sel32.ap[0:32,
                                         (16 + n) * 128:(17 + n) * 128]],
                               [negbcf.ap[0:32, ch * CWS + c5 * W5:
                                          ch * CWS + (c5 + 1) * W5]],
                               PS[2 + c5 % 2].ap[:, 0:W5],
                               ins=(sel32, negbcf),
                               outs=(PS[2 + c5 % 2],))
                            ve(lambda e, c5=c5, ps=PS[2 + c5 % 2]:
                               e.tensor_tensor(
                                   out=yt_c.ap[:, c5 * W5:(c5 + 1) * W5],
                                   in0=h_c.ap[:, c5 * W5:(c5 + 1) * W5],
                                   in1=ps.ap[:, 0:W5], op=ALU.mult),
                               ins=(h_c, PS[2 + c5 % 2]), outs=(yt_c,))
                        if n == 0:
                            ve(lambda e, ch=ch: e.tensor_copy(
                                out=y_c.ap[:, ch * CWS:(ch + 1) * CWS],
                                in_=yt_c.ap[:]),
                               ins=(yt_c,), outs=(y_c,))
                        else:
                            ve(lambda e, ch=ch: e.tensor_tensor(
                                out=y_c.ap[:, ch * CWS:(ch + 1) * CWS],
                                in0=y_c.ap[:, ch * CWS:(ch + 1) * CWS],
                                in1=yt_c.ap[:], op=ALU.add),
                               ins=(yt_c, y_c), outs=(y_c,))
                ve(lambda e: e.tensor_copy(out=ybf_c.ap[:],
                                           in_=y_c.ap[:]),
                   ins=(y_c,), outs=(ybf_c,))
                for r in range(4):
                    dma("se", a1b_in.ap[4 * b_ + r, oidx],
                        ybf_c.ap[:, r * T:(r + 1) * T],
                        ins=(ybf_c,), outs=(a1b_in,))
        AR.release()
        cc("AllToAll", a1b_in, a1b_out, rg=RG8)

        AR.mark()
        WOUT = [AR.alloc(16 * 128, BF16) for _ in range(NJ)]
        for j in range(NJ):
            wslab("W_out", j, 2048, WOUT[j])
        s1p = [AR.alloc(TC, BF16) for _ in range(NOT)]
        yb = AR.alloc(TC, BF16)
        zb2 = AR.alloc(TC, BF16)
        ucb2 = AR.alloc(TC, BF16)
        szf = AR.alloc(TC, F32)
        y2 = AR.alloc(TC, F32)
        x1t = AR.alloc(TC, F32)
        xo = AR.alloc(TC, F32)
        for t in range(4):
            for ot in range(NOT):
                sl_, jj = ot // 2, ot % 2
                dma("sy", yb.ap[:],
                    a1b_out.ap[sl_, jj][:, t * TC:(t + 1) * TC],
                    ins=(a1b_out,), outs=(yb,))
                dma("sy", zb2.ap[:], sc_z.ap[ot, :, t * TC:(t + 1) * TC],
                    ins=(sc_z,), outs=(zb2,))
                dma("sy", ucb2.ap[:],
                    a1_in.ap[sl_, 1, jj][:, t * TC:(t + 1) * TC],
                    ins=(a1_in,), outs=(ucb2,))
                ve(lambda e, ot=ot: e.scalar_tensor_tensor(
                    out=y2.ap[:], in0=ucb2.ap[:],
                    scalar=cols_t["Dp"].ap[:, ot:ot + 1], in1=yb.ap[:],
                    op0=ALU.mult, op1=ALU.add),
                   ins=(ucb2, cols_t["Dp"], yb), outs=(y2,))
                se(lambda e: e.activation(out=szf.ap[:], in_=zb2.ap[:],
                                          func=AF.Silu),
                   ins=(zb2,), outs=(szf,))
                ve(lambda e, ot=ot: e.tensor_tensor(
                    out=s1p[ot].ap[:], in0=y2.ap[:], in1=szf.ap[:],
                    op=ALU.mult), ins=(y2, szf), outs=(s1p[ot],))
                if DBG and "d_y" in dbg_d:
                    dma("se", dbg_d["d_y"][ot * 128:(ot + 1) * 128,
                                           t * TC:(t + 1) * TC],
                        y2.ap[:], ins=(y2,))
            for j in range(NJ):
                mm([WOUT[j].ap[:, kc * 128:(kc + 1) * 128]
                    for kc in range(16)],
                   [s1p[kc].ap[:] for kc in range(16)],
                   PS[4 + j % 2].ap[:, 0:TC], ins=[WOUT[j]] + s1p,
                   outs=(PS[4 + j % 2],))
                dma("sy", x1t.ap[:], sc_x1.ap[j, :, t * TC:(t + 1) * TC],
                    ins=(sc_x1,), outs=(x1t,))
                if DBG and "d_s1" in dbg_d:
                    AR.mark()
                    tq = AR.alloc(TC, F32)
                    ve(lambda e, tq=tq, ps=PS[4 + j % 2]: e.tensor_copy(
                        out=tq.ap[:], in_=ps.ap[:, 0:TC]),
                       ins=(PS[4 + j % 2],), outs=(tq,))
                    dma("se", dbg_d["d_s1"][j * 128:(j + 1) * 128,
                                            t * TC:(t + 1) * TC],
                        tq.ap[:], ins=(tq,))
                    AR.release()
                ve(lambda e, j=j, t=t, ps=PS[4 + j % 2]: e.tensor_tensor(
                    out=xo.ap[:], in0=ps.ap[:, 0:TC],
                    in1=G1[j].ap[:, t * TC:(t + 1) * TC], op=ALU.mult),
                   ins=(PS[4 + j % 2], G1[j]), outs=(xo,))
                ve(lambda e: e.tensor_tensor(out=xo.ap[:], in0=xo.ap[:],
                                             in1=x1t.ap[:], op=ALU.add),
                   ins=(xo, x1t), outs=(xo,))
                dma("se", xout_d[j * 128:(j + 1) * 128,
                                 t * TC:(t + 1) * TC], xo.ap[:],
                    ins=(xo,))
        AR.release()
        AR.release()
        final_toks = []
        for q in DMA_POOL:
            for s in DMA_POOL[q]:
                if p.cnt[s] > 0:
                    final_toks.append((s, p.cnt[s]))
        p.op("ge", lambda e: e.nop(), ins=final_toks, inc=False)

        with nc.Block() as block:
            @block.tensor
            def _(eng):
                _replay(p, "te", eng, sems)

            @block.vector
            def _(eng):
                _replay(p, "ve", eng, sems)

            @block.scalar
            def _(eng):
                _replay(p, "se", eng, sems)

            @block.gpsimd
            def _(eng):
                _replay(p, "ge", eng, sems)

            @block.sync
            def _(eng):
                _replay(p, "sy", eng, sems)

    return nc


_nc_cache = {}


def kernel(**inputs):
    T = np.asarray(inputs["L0_attn_x"]).shape[1] // 4
    if T not in _nc_cache:
        _nc_cache[T] = build_program(T)
    nc = _nc_cache[T]
    in_maps = setup_core_inputs(inputs, T)
    res = run_bass_kernel_spmd(nc, in_maps, core_ids=list(range(NCORES)))
    B = np.asarray(inputs["L0_attn_x"]).shape[0]
    L = np.asarray(inputs["L0_attn_x"]).shape[1]
    out = np.empty((B, L, D), np.float32)
    for c in range(NCORES):
        b, qq = c // 4, c % 4
        out[b, qq * T:(qq + 1) * T, :] = res.results[c]["xout"].T
    return out


# version 02 (final)
# speedup vs baseline: 1.0808x; 1.0808x over previous
import numpy as np
from contextlib import ExitStack

import concourse.bass as bass
import concourse.mybir as mybir
from concourse.bass_utils import run_bass_kernel_spmd

F32 = mybir.dt.float32
BF16 = mybir.dt.bfloat16
I32 = mybir.dt.int32
ALU = mybir.AluOpType
AF = mybir.ActivationFunctionType
AX = mybir.AxisListType

NCORES = 8
D = 1024
NJ = 8
DI = 2048
NOT = 16
DFF = 4096
NFT = 32
NST = 16

DBG = False
DBGN = []

ENGINES = ("te", "ve", "se", "ge", "sy")
DMA_POOL = {"sy": [f"dsy{i}" for i in range(10)],
            "se": [f"dsc{i}" for i in range(10)],
            "ge": [f"dge{i}" for i in range(4)]}
SEMS = ("te", "ve", "se", "ge") + tuple(
    s for pool in DMA_POOL.values() for s in pool)


class Buf:
    __slots__ = ("ap", "wtok", "rtoks")

    def __init__(self, ap, wtok=None, rtoks=None):
        self.ap = ap
        self.wtok = wtok
        self.rtoks = dict(rtoks) if rtoks else {}


class Prog:
    def __init__(self):
        self.sched = {e: [] for e in ENGINES}
        self.cnt = {s: 0 for s in SEMS}
        self.hw = {}
        self.qidx = {q: 0 for q in DMA_POOL}
        self.qopen = {q: False for q in DMA_POOL}
        self.closed = {s: 0 for pool in DMA_POOL.values() for s in pool}
        self.fence = {q: None for q in DMA_POOL}

    def _close_group(self, q):
        pool = DMA_POOL[q]
        s = pool[self.qidx[q]]
        self.closed[s] = self.cnt[s]
        self.qopen[q] = False
        self.qidx[q] = (self.qidx[q] + 1) % len(pool)
        return s, self.closed[s]

    def _resolve_dma_wait(self, s, v):
        for q, pool in DMA_POOL.items():
            if s in pool:
                if self.closed[s] >= v and not (
                        self.qopen[q] and pool[self.qidx[q]] == s and
                        self.cnt[s] > self.closed[s]):
                    return s, self.closed[s] if self.closed[s] >= v else v
                if self.qopen[q] and pool[self.qidx[q]] == s:
                    self._close_group(q)
                    return s, self.closed[s]
                return s, max(self.closed[s], v)
        return s, v

    def op(self, eng, fn, ins=(), outs=(), inc=True, dma=False):
        toks = {}

        def add(t):
            if t is None:
                return
            s, v = t
            if toks.get(s, 0) < v:
                toks[s] = v
        for b in ins:
            add(b.wtok if isinstance(b, Buf) else b)
        for b in outs:
            add(b.wtok)
            for s, v in b.rtoks.items():
                add((s, v))
        waits = []
        for s, v in toks.items():
            if s.startswith("d"):
                s, v = self._resolve_dma_wait(s, v)
            if self.hw.get((eng, s), 0) >= v:
                continue
            self.hw[(eng, s)] = v
            waits.append((s, v))
        tok = None
        if inc:
            if dma:
                q = eng
                pool = DMA_POOL[q]
                if not self.qopen[q]:
                    s_new = pool[self.qidx[q]]
                    if self.cnt[s_new] > 0 and                             self.hw.get((eng, s_new), 0) < self.cnt[s_new]:
                        waits.append((s_new, self.cnt[s_new]))
                        self.hw[(eng, s_new)] = self.cnt[s_new]
                    self.qopen[q] = True
                sem = pool[self.qidx[q]]
                self.cnt[sem] += 16
                tok = (sem, self.cnt[sem])
            else:
                self.cnt[eng] += 1
                tok = (eng, self.cnt[eng])
        self.sched[eng].append((waits, fn, tok))
        if tok is not None:
            for b in ins:
                if isinstance(b, Buf):
                    s, v = tok
                    if b.rtoks.get(s, 0) < v:
                        b.rtoks[s] = v
            for b in outs:
                b.wtok = tok
                b.rtoks = {}
        return tok


def _replay(prog, eng_name, handle, sems):
    for waits, fn, tok in prog.sched[eng_name]:
        for s, v in waits:
            handle.wait_ge(sems[s], v)
        ins = fn(handle)
        if tok is not None:
            s, _ = tok
            ins.then_inc(sems[s], 16 if s.startswith("d") else 1)


class Arena:
    def __init__(self, handle, nbytes):
        self.h = handle
        self.size = nbytes
        self.cur = 0
        self.stack = [(0, [])]
        self.pending = {}

    def alloc(self, cols, dtype):
        esz = 2 if dtype == BF16 else 4
        off = (self.cur + esz - 1) // esz * esz
        assert off + cols * esz <= self.size, \
            f"arena overflow {off}+{cols * esz}>{self.size}"
        self.cur = off + cols * esz
        ap = self.h[:].bitcast(dtype)[:, off // esz: off // esz + cols]
        b = Buf(ap, rtoks=self.pending)
        self.stack[-1][1].append(b)
        return b

    def mark(self):
        self.stack.append((self.cur, []))

    def release(self):
        start, bufs = self.stack.pop()
        for b in bufs:
            if b.wtok:
                s, v = b.wtok
                if self.pending.get(s, 0) < v:
                    self.pending[s] = v
            for s, v in b.rtoks.items():
                if self.pending.get(s, 0) < v:
                    self.pending[s] = v
        self.cur = start


def _feat_cols(v, nt):
    return np.ascontiguousarray(np.asarray(v, np.float32).reshape(nt, 128).T)


COLNAMES = [("ln1_g", NJ), ("ln1_b", NJ), ("bq", NJ), ("bk", NJ), ("bv", NJ),
            ("bg", NJ), ("bo", NJ), ("ln2_g", NJ), ("ln2_b", NJ),
            ("b1", NFT), ("b2", NJ), ("cw0", NFT), ("cw1", NFT),
            ("cw2", NFT), ("cb", NFT), ("c1w0", NOT), ("c1w1", NOT),
            ("c1w2", NOT), ("c1w3", NOT), ("c1b", NOT), ("bdt", NOT),
            ("Dp", NOT), ("bias1f", NJ)]
WEIGHTS = [("Wq", 8, D, "bf"), ("Wk", 8, D, "bf"), ("Wv", 8, D, "bf"),
           ("Wg", 8, D, "bf"), ("Wo", 8, D, "bf"), ("W1", 32, D, "bf"),
           ("W2", 8, DFF, "bf"), ("w1T", 32, D, "f32"),
           ("w2T", 8, DFF, "f32"), ("W_in", 32, D, "bf"),
           ("W_dt", 16, 128, "bf"), ("W_out", 8, DI, "bf"),
           ("Kf", 8, D, "f32")]


def setup_core_inputs(inputs, T):
    L = 4 * T
    x = np.asarray(inputs["L0_attn_x"], np.float32)
    f32 = np.float32
    import ml_dtypes
    bf16 = ml_dtypes.bfloat16

    def wlay(w, bf=False):
        K, M = np.asarray(w).shape
        a = np.ascontiguousarray(
            np.asarray(w, f32).reshape(K, M // 128, 128).transpose(1, 0, 2))
        return np.ascontiguousarray(a.astype(bf16)) if bf else a

    g = {
        "Wq": wlay(inputs["L0_attn_Wq"], True),
        "Wk": wlay(inputs["L0_attn_Wk"], True),
        "Wv": wlay(inputs["L0_attn_Wv"], True),
        "Wg": wlay(inputs["L0_attn_Wg"], True),
        "Wo": wlay(np.asarray(inputs["L0_attn_Wo"], f32) * f32(0.125),
                   True),
        "W1": wlay(inputs["L0_attn_W1"], True),
        "W2": wlay(inputs["L0_attn_W2"], True),
        "w1T": wlay(np.ascontiguousarray(np.asarray(inputs["L0_ffn_w1"]).T)),
        "w2T": wlay(np.ascontiguousarray(np.asarray(inputs["L0_ffn_w2"]).T)),
        "W_in": wlay(inputs["L1_attn_W_in"], True),
        "W_dt": wlay(inputs["L1_attn_W_dt"], True),
        "W_out": wlay(inputs["L1_attn_W_out"], True),
        "Kf": wlay(inputs["L1_ffn_kernel"]),
    }
    Wx = np.ascontiguousarray(
        np.asarray(inputs["L1_attn_W_x"], f32).astype(bf16))
    alora = np.ascontiguousarray(
        np.asarray(inputs["L1_ffn_a"], f32).astype(bf16))
    blora = np.ascontiguousarray(
        np.asarray(inputs["L1_ffn_b"], f32).astype(bf16))
    cw0 = np.asarray(inputs["L0_ffn_conv_w"], f32)
    c1w = np.asarray(inputs["L1_attn_conv_w"], f32)
    cols = {
        "ln1_g": _feat_cols(inputs["L0_attn_ln1_g"], NJ),
        "ln1_b": _feat_cols(inputs["L0_attn_ln1_b"], NJ),
        "bq": _feat_cols(inputs["L0_attn_bq"], NJ),
        "bk": _feat_cols(inputs["L0_attn_bk"], NJ),
        "bv": _feat_cols(inputs["L0_attn_bv"], NJ),
        "bg": _feat_cols(inputs["L0_attn_bg"], NJ),
        "bo": _feat_cols(inputs["L0_attn_bo"], NJ),
        "ln2_g": _feat_cols(inputs["L0_attn_ln2_g"], NJ),
        "ln2_b": _feat_cols(inputs["L0_attn_ln2_b"], NJ),
        "b1": _feat_cols(inputs["L0_attn_b1"], NFT),
        "b2": _feat_cols(inputs["L0_attn_b2"], NJ),
        "cw0": _feat_cols(cw0[:, 0, 0], NFT),
        "cw1": _feat_cols(cw0[:, 0, 1], NFT),
        "cw2": _feat_cols(cw0[:, 0, 2], NFT),
        "cb": _feat_cols(inputs["L0_ffn_conv_b"], NFT),
        "c1w0": _feat_cols(c1w[:, 0], NOT),
        "c1w1": _feat_cols(c1w[:, 1], NOT),
        "c1w2": _feat_cols(c1w[:, 2], NOT),
        "c1w3": _feat_cols(c1w[:, 3], NOT),
        "c1b": _feat_cols(inputs["L1_attn_conv_b"], NOT),
        "bdt": _feat_cols(inputs["L1_attn_b_dt"], NOT),
        "Dp": _feat_cols(inputs["L1_attn_D_param"], NOT),
        "bias1f": _feat_cols(inputs["L1_ffn_bias"], NJ),
    }
    ident = np.eye(128, dtype=bf16)
    sel32 = np.zeros((32, 32 * 128), dtype=bf16)
    for n in range(32):
        sel32[n, n * 128:(n + 1) * 128] = 1
    dl = np.asarray(inputs["L0_attn_decay_logit"], f32)
    Al = np.asarray(inputs["L1_attn_A_log"], f32)

    ins = []
    for c in range(NCORES):
        b, qq = c // 4, c % 4
        m = {}
        m["x0"] = np.ascontiguousarray(x[b, qq * T:(qq + 1) * T].T)
        hl = np.zeros((2, D), f32)
        if qq * T - 1 >= 0:
            hl[0] = x[b, qq * T - 1]
        if (qq + 1) * T < L:
            hl[1] = x[b, (qq + 1) * T]
        m["x0h"] = np.ascontiguousarray(hl.T)
        for k, v in g.items():
            m[k] = v
        m["W_x"] = Wx
        m["a_lora"] = alora
        m["b_lora"] = blora
        for k, v in cols.items():
            m["c_" + k] = v
        dcols = np.zeros((128, 1), f32)
        for p_ in range(128):
            dcols[p_, 0] = dl[(c * 128 + p_) // 64]
        m["my_decay"] = dcols
        acols = np.zeros((128, 2 * NST), f32)
        for oi in range(2):
            go = 2 * c + oi
            acols[:, oi * NST:(oi + 1) * NST] = Al[go * 128:(go + 1) * 128]
        m["my_Alog"] = np.ascontiguousarray(acols)
        hm = np.zeros((128, 4), f32)
        if qq > 0:
            hm[:, qq - 1] = 1.0
        m["halo_mask"] = hm
        m["ident"] = ident
        m["sel32"] = sel32
        m["sgn_bc"] = np.ones((32, 1), f32)
        ins.append(m)
    return ins


def build_program(T):
    TC = T // 4
    L = 4 * T
    nkT = T // 128
    nc = bass.Bass("TRN2", target_bir_lowering=False)
    din = {}

    def dr(name, shape, dt=F32):
        din[name] = nc.dram_tensor(name, list(shape), dt,
                                   kind="ExternalInput").ap()

    dr("x0", (D, T))
    dr("x0h", (D, 2))
    for w, nb, K, dt_ in WEIGHTS:
        dr(w, (nb, K, 128), BF16 if dt_ == "bf" else F32)
    dr("W_x", (DI, 160), BF16)
    dr("a_lora", (D, 16), BF16)
    dr("b_lora", (16, D), BF16)
    for k, nt in COLNAMES:
        dr("c_" + k, (128, nt))
    dr("my_decay", (128, 1))
    dr("my_Alog", (128, 2 * NST))
    dr("halo_mask", (128, 4))
    dr("ident", (128, 128), BF16)
    dr("sel32", (32, 32 * 128), BF16)
    dr("sgn_bc", (32, 1))

    xout_d = nc.dram_tensor("xout", [D, T], F32, kind="ExternalOutput").ap()
    dbg_d = {}
    if DBG:
        for nm, shape in [("d_xn0", (D, T)), ("d_y1", (D, T)),
                          ("d_kv", (D, T)), ("d_qg", (D, T)),
                          ("d_outs", (D, T)), ("d_x2", (D, T)),
                          ("d_s0", (D, T)), ("d_qin", (D, T)),
                          ("d_h", (DFF, T)), ("d_hc", (DFF, T)),
                          ("d_g0", (D, T)), ("d_x1", (D, T)),
                          ("d_uc", (DI, T)), ("d_dt", (DI, T)),
                          ("d_bc", (32, T)), ("d_y", (DI, T)),
                          ("d_g1", (D, T)), ("d_s1", (D, T))]:
            if nm in DBGN or "all" in DBGN:
                dbg_d[nm] = nc.dram_tensor(nm, list(shape), F32,
                                           kind="ExternalOutput").ap()

    def idr(name, shape, dt=F32):
        return Buf(nc.dram_tensor(name, list(shape), dt).ap())

    sc_s0 = idr("sc_s0", (NJ, 128, T))
    sc_hc = idr("sc_hc", (NFT, 128, T), BF16)
    sc_z = idr("sc_z", (NOT, 128, T), BF16)
    sc_x1 = idr("sc_x1", (NJ, 128, T))
    a0_in = idr("a0_in", (8, 2, 128, T), BF16)
    a0_out = idr("a0_out", (8, 2, 128, T), BF16)
    a0b_in = idr("a0b_in", (8, 128, T), BF16)
    a0b_out = idr("a0b_out", (8, 128, T), BF16)
    a1_in = idr("a1_in", (8, 2, 2, 128, T), BF16)
    a1_out = idr("a1_out", (8, 2, 2, 128, T), BF16)
    a1b_in = idr("a1b_in", (8, 2, 128, T), BF16)
    a1b_out = idr("a1b_out", (8, 2, 128, T), BF16)
    agbc_in = idr("agbc_in", (32, T), BF16)
    agbc_out = idr("agbc_out", (8, 32, T), BF16)
    agh_in = idr("agh_in", (NJ, 128, 3))
    agh_out = idr("agh_out", (4, NJ, 128, 3))
    RG = [[0, 1, 2, 3], [4, 5, 6, 7]]
    RG8 = [[0, 1, 2, 3, 4, 5, 6, 7]]

    p = Prog()
    es = ExitStack()
    with es:
        AR_h = es.enter_context(nc.sbuf_tensor("AR", [128, 152 * 256], F32))
        AR = Arena(AR_h, 152 * 1024)
        C_h = es.enter_context(nc.sbuf_tensor("CR", [128, 14 * 256], F32))
        CR = Arena(C_h, 14 * 1024)
        RW_h = es.enter_context(nc.sbuf_tensor("RW", [128, 5 * T], F32))
        RS = [Buf(RW_h[0:1, i * T:(i + 1) * T]) for i in range(5)]

        PS = [Buf(es.enter_context(
            nc.psum_tensor(f"PS{i}", [128, 512], F32)).ap())
            for i in range(8)]
        sems = {s: es.enter_context(nc.semaphore("s_" + s)) for s in SEMS}

        def dma(stream, out, in_, outs=(), ins=()):
            return p.op(stream, lambda e, o=out, i=in_:
                        e.dma_start(out=o, in_=i),
                        ins=ins, outs=outs, dma=True)

        def mm(lhs_list, rhs_list, out_ap, ins=(), outs=(), accum=False,
               final=True):
            n = len(lhs_list)
            tok = None
            for i in range(n):
                first = (i == 0) and not accum
                stop = final and (i == n - 1)
                tok = p.op("te",
                           lambda e, l=lhs_list[i], r=rhs_list[i], st=first,
                           sp=stop: e.matmul(out=out_ap, lhsT=l, rhs=r,
                                             start=st, stop=sp),
                           ins=(tuple(ins) + tuple(
                               b.wtok for b in outs if b.wtok) + tuple(
                               (s, v) for b in outs
                               for s, v in b.rtoks.items()))
                           if (i == 0 or i == n - 1) else (),
                           outs=outs if i == n - 1 else (),
                           inc=(i == n - 1))
            return tok

        def ve(fn, ins=(), outs=()):
            return p.op("ve", fn, ins=ins, outs=outs)

        def ge_(fn, ins=(), outs=()):
            return p.op("ge", fn, ins=ins, outs=outs)

        def se(fn, ins=(), outs=()):
            return p.op("se", fn, ins=ins, outs=outs)

        def te(fn, ins=(), outs=()):
            return p.op("te", fn, ins=ins, outs=outs)

        from concourse.aot_env import allow_ici_all_to_all

        def cc(kind, inb, outb, rg=None):
            rgv = rg if rg is not None else RG
            def emit(e, k=kind, a=inb, b=outb, rgv=rgv):
                with allow_ici_all_to_all():
                    return e.collective_compute(
                        k, ALU.bypass, replica_groups=rgv,
                        ins=[a.ap.opt()], outs=[b.ap.opt()])
            return p.op("ge", emit, ins=(inb,), outs=(outb,))

        ones_c = CR.alloc(128, F32)
        onesb_col = CR.alloc(1, BF16)
        ident_b = CR.alloc(128, BF16)
        sel32 = CR.alloc(32 * 128, BF16)
        ve(lambda e: e.memset(ones_c.ap[0:1, :], 1.0), outs=(ones_c,))
        ve(lambda e: e.memset(onesb_col.ap[:, 0:1], 1.0), outs=(onesb_col,))
        dma("sy", ident_b.ap[:], din["ident"], outs=(ident_b,))
        dma("sy", sel32.ap[0:32, :], din["sel32"], outs=(sel32,))
        cols_t = {}
        for k, nt in COLNAMES:
            cols_t[k] = CR.alloc(nt, F32)
            dma("sy", cols_t[k].ap[:], din["c_" + k], outs=(cols_t[k],))
        myA = CR.alloc(2 * NST, F32)
        negA = CR.alloc(2 * NST, F32)
        mydec = CR.alloc(1, F32)
        halo_m = CR.alloc(4, F32)
        sgnbc = CR.alloc(1, F32)
        dma("sy", sgnbc.ap[0:32, 0:1], din["sgn_bc"], outs=(sgnbc,))
        dma("sy", myA.ap[:], din["my_Alog"], outs=(myA,))
        dma("sy", mydec.ap[:], din["my_decay"], outs=(mydec,))
        dma("sy", halo_m.ap[:], din["halo_mask"], outs=(halo_m,))
        se(lambda e: e.activation(out=negA.ap[:], in_=myA.ap[:],
                                  func=AF.Exp), ins=(myA,), outs=(negA,))
        ve(lambda e: e.tensor_scalar(out=negA.ap[:], in0=negA.ap[:],
                                     scalar1=-1.0, scalar2=None,
                                     op0=ALU.mult), ins=(negA,),
           outs=(negA,))
        ident_f = CR.alloc(128, F32)
        ve(lambda e: e.tensor_copy(out=ident_f.ap[:], in_=ident_b.ap[:]),
           ins=(ident_b,), outs=(ident_f,))
        mxsb = CR.alloc(128, F32)
        mxc = CR.alloc(nkT + 1, F32)
        x0h_t = CR.alloc(2 * NJ, F32)
        for j in range(NJ):
            dma("sy", x0h_t.ap[:, 2 * j:2 * j + 2],
                din["x0h"][j * 128:(j + 1) * 128, :], outs=(x0h_t,))

        def srow(w):
            return CR.alloc(w, F32)

        def dbg_dump(name, tiles, width=T, scale=None):
            if not (DBG and name in dbg_d):
                return
            AR.mark()
            tmp = AR.alloc(width, F32)
            for j, bb in enumerate(tiles):
                if scale is None:
                    ve(lambda e, bb=bb: e.tensor_copy(
                        out=tmp.ap[:, 0:width], in_=bb.ap[:, 0:width]),
                       ins=(bb,), outs=(tmp,))
                else:
                    ve(lambda e, bb=bb: e.tensor_scalar(
                        out=tmp.ap[:, 0:width], in0=bb.ap[:, 0:width],
                        scalar1=scale, scalar2=None, op0=ALU.mult),
                       ins=(bb,), outs=(tmp,))
                dma("se", dbg_d[name][j * 128:(j + 1) * 128, 0:width],
                    tmp.ap[:, 0:width], ins=(tmp,))
            AR.release()

        def ln_stats(fill_bf, nt, width, mu_b, r_b):
            AR.mark()
            xb = AR.alloc(512, BF16)
            xs = AR.alloc(512, BF16)
            ntc = (width + 511) // 512
            for t in range(ntc):
                w = min(512, width - t * 512)
                for j in range(nt):
                    fill_bf(j, t * 512, w, xb)
                    se(lambda e, w=w: e.activation(
                        out=xs.ap[:, 0:w], in_=xb.ap[:, 0:w],
                        func=AF.Square), ins=(xb,), outs=(xs,))
                    mm([onesb_col.ap[:, 0:1]], [xb.ap[:, 0:w]],
                       PS[6].ap[0:1, 0:w], ins=(xb, onesb_col),
                       outs=(PS[6],), accum=(j > 0), final=(j == nt - 1))
                    mm([onesb_col.ap[:, 0:1]], [xs.ap[:, 0:w]],
                       PS[7].ap[0:1, 0:w], ins=(xs, onesb_col),
                       outs=(PS[7],), accum=(j > 0), final=(j == nt - 1))
                se(lambda e, t=t, w=w: e.copy(
                    out=mu_b.ap[0:1, t * 512:t * 512 + w],
                    in_=PS[6].ap[0:1, 0:w]), ins=(PS[6],), outs=(mu_b,))
                se(lambda e, t=t, w=w: e.copy(
                    out=r_b.ap[0:1, t * 512:t * 512 + w],
                    in_=PS[7].ap[0:1, 0:w]), ins=(PS[7],), outs=(r_b,))
            nd = float(nt * 128)
            ve(lambda e: e.tensor_scalar(out=mu_b.ap[0:1, 0:width],
                                         in0=mu_b.ap[0:1, 0:width],
                                         scalar1=1.0 / nd, scalar2=None,
                                         op0=ALU.mult), ins=(mu_b,),
               outs=(mu_b,))
            msq = AR.alloc(width, F32)
            se(lambda e: e.activation(out=msq.ap[0:1, 0:width],
                                      in_=mu_b.ap[0:1, 0:width],
                                      func=AF.Square), ins=(mu_b,),
               outs=(msq,))
            ve(lambda e: e.tensor_scalar(out=r_b.ap[0:1, 0:width],
                                         in0=r_b.ap[0:1, 0:width],
                                         scalar1=1.0 / nd, scalar2=None,
                                         op0=ALU.mult), ins=(r_b,),
               outs=(r_b,))
            ve(lambda e: e.tensor_tensor(out=r_b.ap[0:1, 0:width],
                                         in0=r_b.ap[0:1, 0:width],
                                         in1=msq.ap[0:1, 0:width],
                                         op=ALU.subtract),
               ins=(r_b, msq), outs=(r_b,))
            ve(lambda e: e.tensor_scalar(out=r_b.ap[0:1, 0:width],
                                         in0=r_b.ap[0:1, 0:width],
                                         scalar1=1e-5, scalar2=None,
                                         op0=ALU.add), ins=(r_b,),
               outs=(r_b,))
            se(lambda e: e.activation(out=r_b.ap[0:1, 0:width],
                                      in_=r_b.ap[0:1, 0:width],
                                      func=AF.Sqrt), ins=(r_b,),
               outs=(r_b,))
            ve(lambda e: e.reciprocal(out=r_b.ap[0:1, 0:width],
                                      in_=r_b.ap[0:1, 0:width]),
               ins=(r_b,), outs=(r_b,))
            AR.release()

        def bcast(row_buf, lo, w, ps_buf):
            mm([ones_c.ap[0:1, :]], [row_buf.ap[0:1, lo:lo + w]],
               ps_buf.ap[:, 0:w], ins=(row_buf, ones_c), outs=(ps_buf,))

        def wslab(dramname, blk, ncols, dst):
            nkc = ncols // 128
            dma("sy",
                dst.ap[:, 0:ncols].rearrange("p (kc c) -> p kc c",
                                             kc=nkc),
                din[dramname][blk].rearrange("(kc p) c -> p kc c", p=128),
                outs=(dst,))

        mu0, r0 = RS[0], RS[1]
        mu0h = srow(2)
        r0h = srow(2)
        AR.mark()
        KV = [AR.alloc(T, BF16) for _ in range(NJ)]
        QG = [AR.alloc(T, BF16) for _ in range(NJ)]
        AR.mark()
        Y1 = [AR.alloc(T, BF16) for _ in range(NJ)]
        AR.mark()
        XN0 = [AR.alloc(T, BF16) for _ in range(NJ)]
        xfd = [AR.alloc(512, F32) for _ in range(2)]

        def fill_from_x0(j, lo, w, xb):
            xf = xfd[j % 2]
            dma("sy", xf.ap[:, 0:w],
                din["x0"][j * 128:(j + 1) * 128, lo:lo + w], outs=(xf,))
            ve(lambda e, w=w, xf=xf: e.tensor_copy(out=xb.ap[:, 0:w],
                                                   in_=xf.ap[:, 0:w]),
               ins=(xf,), outs=(xb,))
        ln_stats(fill_from_x0, NJ, T, mu0, r0)

        def fill_from_cr(src, percol=2):
            def f(j, lo, w, xb):
                ve(lambda e, j=j, lo=lo, w=w: e.tensor_copy(
                    out=xb.ap[:, 0:w],
                    in_=src.ap[:, percol * j + lo:percol * j + lo + w]),
                   ins=(src,), outs=(xb,))
            return f
        ln_stats(fill_from_cr(x0h_t), NJ, 2, mu0h, r0h)

        for t in range(4):
            bcast(mu0, t * TC, TC, PS[6])
            bcast(r0, t * TC, TC, PS[7])
            AR.mark()
            tmpa = AR.alloc(TC, F32)
            for j in range(NJ):
                dma("sy", tmpa.ap[:],
                    din["x0"][j * 128:(j + 1) * 128, t * TC:(t + 1) * TC],
                    outs=(tmpa,))
                ve(lambda e: e.tensor_tensor(
                    out=tmpa.ap[:], in0=tmpa.ap[:], in1=PS[6].ap[:, 0:TC],
                    op=ALU.subtract), ins=(tmpa, PS[6]), outs=(tmpa,))
                ve(lambda e, j=j, t=t: e.tensor_tensor(
                    out=XN0[j].ap[:, t * TC:(t + 1) * TC], in0=tmpa.ap[:],
                    in1=PS[7].ap[:, 0:TC], op=ALU.mult),
                   ins=(tmpa, PS[7]), outs=(XN0[j],))
            AR.release()
        dbg_dump("d_xn0", XN0)
        mu1, r1 = RS[2], RS[3]

        def fill_from_tiles(tiles):
            def f(j, lo, w, xb):
                ve(lambda e, j=j, lo=lo, w=w: e.tensor_copy(
                    out=xb.ap[:, 0:w], in_=tiles[j].ap[:, lo:lo + w]),
                   ins=(tiles[j],), outs=(xb,))
            return f
        ln_stats(fill_from_tiles(XN0), NJ, T, mu1, r1)
        for t in range(4):
            bcast(mu1, t * TC, TC, PS[6])
            bcast(r1, t * TC, TC, PS[7])
            AR.mark()
            tmpb = AR.alloc(TC, F32)
            for j in range(NJ):
                ve(lambda e, j=j, t=t: e.tensor_tensor(
                    out=tmpb.ap[:], in0=XN0[j].ap[:, t * TC:(t + 1) * TC],
                    in1=PS[6].ap[:, 0:TC], op=ALU.subtract),
                   ins=(XN0[j], PS[6]), outs=(tmpb,))
                ve(lambda e: e.tensor_tensor(out=tmpb.ap[:], in0=tmpb.ap[:],
                                             in1=PS[7].ap[:, 0:TC],
                                             op=ALU.mult),
                   ins=(tmpb, PS[7]), outs=(tmpb,))
                ve(lambda e, j=j, t=t: e.tensor_scalar(
                    out=Y1[j].ap[:, t * TC:(t + 1) * TC], in0=tmpb.ap[:],
                    scalar1=cols_t["ln1_g"].ap[:, j:j + 1],
                    scalar2=cols_t["ln1_b"].ap[:, j:j + 1],
                    op0=ALU.mult, op1=ALU.add),
                   ins=(tmpb, cols_t["ln1_g"], cols_t["ln1_b"]),
                   outs=(Y1[j],))
            AR.release()
        AR.release()
        dbg_dump("d_y1", Y1)

        AR.mark()
        slq = [AR.alloc(1024, BF16) for _ in range(2)]
        slk = [AR.alloc(1024, BF16) for _ in range(2)]
        slv = [AR.alloc(1024, BF16) for _ in range(2)]
        slg = [AR.alloc(1024, BF16) for _ in range(2)]
        kf = [AR.alloc(TC, F32) for _ in range(2)]
        vf = [AR.alloc(TC, F32) for _ in range(2)]
        qf = [AR.alloc(TC, F32) for _ in range(2)]
        gf = [AR.alloc(TC, F32) for _ in range(2)]
        for j in range(NJ):
            jp = j % 2
            for w, dst in [("Wq", slq[jp]), ("Wk", slk[jp]),
                           ("Wv", slv[jp]), ("Wg", slg[jp])]:
                wslab(w, j, 1024, dst)
            for t in range(4):
                tp_ = t % 2
                for (sl, psi, bias, dstl, func) in [
                        (slk[jp], 0, "bk", kf, AF.Identity),
                        (slv[jp], 1, "bv", vf, AF.Identity),
                        (slq[jp], 2, "bq", qf, AF.Identity),
                        (slg[jp], 3, "bg", gf, AF.Sigmoid)]:
                    dst = dstl[tp_]
                    mm([sl.ap[:, kc * 128:(kc + 1) * 128]
                        for kc in range(8)],
                       [Y1[kc].ap[:, t * TC:(t + 1) * TC]
                        for kc in range(8)],
                       PS[psi].ap[:, 0:TC], ins=[sl] + Y1,
                       outs=(PS[psi],))
                    se(lambda e, psi=psi, dst=dst, bias=bias, func=func,
                       j=j: e.activation(out=dst.ap[:],
                                         in_=PS[psi].ap[:, 0:TC],
                                         func=func,
                                         bias=cols_t[bias].ap[:, j:j + 1]),
                       ins=(PS[psi], cols_t[bias]), outs=(dst,))
                ve(lambda e, j=j, t=t, a=kf[tp_], b=vf[tp_]:
                   e.tensor_tensor(
                    out=KV[j].ap[:, t * TC:(t + 1) * TC], in0=a.ap[:],
                    in1=b.ap[:], op=ALU.mult), ins=(kf[tp_], vf[tp_]),
                   outs=(KV[j],))
                ve(lambda e, j=j, t=t, a=qf[tp_], b=gf[tp_]:
                   e.tensor_tensor(
                    out=QG[j].ap[:, t * TC:(t + 1) * TC], in0=a.ap[:],
                    in1=b.ap[:], op=ALU.mult), ins=(qf[tp_], gf[tp_]),
                   outs=(QG[j],))
        AR.release()
        AR.release()
        dbg_dump("d_kv", KV)
        dbg_dump("d_qg", QG)
        for sl_ in range(8):
            dma("se", a0_in.ap[sl_, 0], KV[sl_].ap[:],
                ins=(KV[sl_],), outs=(a0_in,))
            dma("se", a0_in.ap[sl_, 1], QG[sl_].ap[:],
                ins=(QG[sl_],), outs=(a0_in,))
        cc("AllToAll", a0_in, a0_out, rg=RG8)
        AR.release()

        AR.mark()
        dec2 = CR.alloc(1, F32)
        se(lambda e: e.activation(out=dec2.ap[:], in_=mydec.ap[:],
                                  func=AF.Sigmoid), ins=(mydec,),
           outs=(dec2,))
        kv_s = AR.alloc(L, BF16)
        qg_s = AR.alloc(L, BF16)
        out_s = AR.alloc(L, BF16)
        CW = min(2048, L)
        NQ = L // CW
        onesq = AR.alloc(CW, F32)
        decbc = AR.alloc(CW, F32)
        stb = [AR.alloc(CW, F32) for _ in range(2)]
        ve(lambda e: e.memset(onesq.ap[:], 1.0), outs=(onesq,))
        ve(lambda e: e.tensor_scalar(
            out=decbc.ap[:], in0=onesq.ap[:],
            scalar1=dec2.ap[:, 0:1], scalar2=None, op0=ALU.mult),
           ins=(onesq, dec2), outs=(decbc,))
        for b_ in range(2):
            for r in range(4):
                dma("sy", kv_s.ap[:, r * T:(r + 1) * T],
                    a0_out.ap[4 * b_ + r, 0], ins=(a0_out,), outs=(kv_s,))
                dma("sy", qg_s.ap[:, r * T:(r + 1) * T],
                    a0_out.ap[4 * b_ + r, 1], ins=(a0_out,), outs=(qg_s,))
            for q_ in range(NQ):
                dst = stb[q_ % 2]
                init = 0.0 if q_ == 0 else stb[(q_ - 1) % 2].ap[:,
                                                                CW - 1:CW]
                extra = () if q_ == 0 else (stb[(q_ - 1) % 2],)
                ve(lambda e, q_=q_, dst=dst, init=init:
                   e.tensor_tensor_scan(
                       out=dst.ap[:], data0=decbc.ap[:],
                       data1=kv_s.ap[:, q_ * CW:(q_ + 1) * CW],
                       initial=init, op0=ALU.mult, op1=ALU.add),
                   ins=(decbc, kv_s) + extra, outs=(dst,))
                ve(lambda e, q_=q_, dst=dst: e.tensor_tensor(
                    out=out_s.ap[:, q_ * CW:(q_ + 1) * CW],
                    in0=qg_s.ap[:, q_ * CW:(q_ + 1) * CW],
                    in1=dst.ap[:], op=ALU.mult),
                   ins=(qg_s, dst), outs=(out_s,))
            for r in range(4):
                dma("se", a0b_in.ap[4 * b_ + r],
                    out_s.ap[:, r * T:(r + 1) * T],
                    ins=(out_s,), outs=(a0b_in,))
        cc("AllToAll", a0b_in, a0b_out, rg=RG8)
        AR.release()

        AR.mark()
        S2 = [AR.alloc(T, F32) for _ in range(NJ)]
        AR.mark()
        OUTS = [AR.alloc(T, BF16) for _ in range(NJ)]
        for j in range(NJ):
            dma("sy", OUTS[j].ap[:], a0b_out.ap[j],
                ins=(a0b_out,), outs=(OUTS[j],))
        dbg_dump("d_outs", OUTS, scale=0.125)
        slab2 = [AR.alloc(1024, BF16) for _ in range(2)]
        xt2b = [AR.alloc(TC, F32) for _ in range(2)]
        for j in range(NJ):
            slab = slab2[j % 2]
            wslab("Wo", j, 1024, slab)
            for t in range(4):
                xt = xt2b[t % 2]
                mm([slab.ap[:, kc * 128:(kc + 1) * 128]
                    for kc in range(8)],
                   [OUTS[kc].ap[:, t * TC:(t + 1) * TC]
                    for kc in range(8)],
                   PS[t % 2].ap[:, 0:TC], ins=[slab] + OUTS,
                   outs=(PS[t % 2],))
                bcast(mu0, t * TC, TC, PS[6])
                bcast(r0, t * TC, TC, PS[7])
                dma("sy", xt.ap[:],
                    din["x0"][j * 128:(j + 1) * 128, t * TC:(t + 1) * TC],
                    outs=(xt,))
                ve(lambda e, xt=xt: e.tensor_tensor(
                    out=xt.ap[:], in0=xt.ap[:],
                    in1=PS[6].ap[:, 0:TC], op=ALU.subtract),
                   ins=(xt, PS[6]), outs=(xt,))
                ve(lambda e, xt=xt: e.tensor_tensor(
                    out=xt.ap[:], in0=xt.ap[:],
                    in1=PS[7].ap[:, 0:TC], op=ALU.mult),
                   ins=(xt, PS[7]), outs=(xt,))
                ve(lambda e, j=j, t=t, xt=xt, ps=PS[t % 2]:
                   e.scalar_tensor_tensor(
                       out=S2[j].ap[:, t * TC:(t + 1) * TC],
                       in0=ps.ap[:, 0:TC],
                       scalar=cols_t["bo"].ap[:, j:j + 1], in1=xt.ap[:],
                       op0=ALU.add, op1=ALU.add),
                   ins=(PS[t % 2], cols_t["bo"], xt), outs=(S2[j],))
        AR.release()
        dbg_dump("d_x2", S2)
        mu2, r2 = RS[2], RS[3]
        ln_stats(fill_from_tiles(S2), NJ, T, mu2, r2)
        for t in range(4):
            AR.mark()
            h2b = [AR.alloc(TC, BF16) for _ in range(NJ)]
            bcast(mu2, t * TC, TC, PS[6])
            bcast(r2, t * TC, TC, PS[7])
            tmpc = AR.alloc(TC, F32)
            for j in range(NJ):
                ve(lambda e, j=j, t=t: e.tensor_tensor(
                    out=tmpc.ap[:], in0=S2[j].ap[:, t * TC:(t + 1) * TC],
                    in1=PS[6].ap[:, 0:TC], op=ALU.subtract),
                   ins=(S2[j], PS[6]), outs=(tmpc,))
                ve(lambda e: e.tensor_tensor(out=tmpc.ap[:], in0=tmpc.ap[:],
                                             in1=PS[7].ap[:, 0:TC],
                                             op=ALU.mult),
                   ins=(tmpc, PS[7]), outs=(tmpc,))
                ve(lambda e, j=j: e.tensor_scalar(
                    out=h2b[j].ap[:], in0=tmpc.ap[:],
                    scalar1=cols_t["ln2_g"].ap[:, j:j + 1],
                    scalar2=cols_t["ln2_b"].ap[:, j:j + 1],
                    op0=ALU.mult, op1=ALU.add),
                   ins=(tmpc, cols_t["ln2_g"], cols_t["ln2_b"]),
                   outs=(h2b[j],))
            ff1 = [AR.alloc(TC, BF16) for _ in range(NFT)]
            sla3d = [AR.alloc(4096, BF16) for _ in range(2)]
            for ot in range(NFT):
                sla3 = sla3d[ot % 2]
                wslab("W1", ot, 1024, sla3)
                mm([sla3.ap[:, kc * 128:(kc + 1) * 128]
                    for kc in range(8)],
                   [h2b[kc].ap[:] for kc in range(8)],
                   PS[ot % 4].ap[:, 0:TC], ins=[sla3] + h2b,
                   outs=(PS[ot % 4],))
                se(lambda e, ot=ot, ps=PS[ot % 4]: e.activation(
                    out=ff1[ot].ap[:], in_=ps.ap[:, 0:TC], func=AF.Gelu,
                    bias=cols_t["b1"].ap[:, ot:ot + 1]),
                   ins=(PS[ot % 4], cols_t["b1"]), outs=(ff1[ot],))
            for j in range(NJ):
                sla3 = sla3d[j % 2]
                wslab("W2", j, 4096, sla3)
                mm([sla3.ap[:, kc * 128:(kc + 1) * 128]
                    for kc in range(32)],
                   [ff1[kc].ap[:] for kc in range(32)],
                   PS[4 + j % 2].ap[:, 0:TC], ins=[sla3] + ff1,
                   outs=(PS[4 + j % 2],))
                ve(lambda e, j=j, t=t, ps=PS[4 + j % 2]:
                   e.scalar_tensor_tensor(
                       out=S2[j].ap[:, t * TC:(t + 1) * TC],
                       in0=ps.ap[:, 0:TC],
                       scalar=cols_t["b2"].ap[:, j:j + 1],
                       in1=S2[j].ap[:, t * TC:(t + 1) * TC],
                       op0=ALU.add, op1=ALU.add),
                   ins=(PS[4 + j % 2], cols_t["b2"], S2[j]), outs=(S2[j],))
            AR.release()
        dbg_dump("d_s0", S2)
        for j in range(NJ):
            dma("se", sc_s0.ap[j], S2[j].ap[:], ins=(S2[j],),
                outs=(sc_s0,))
        AR.release()

        def stream_reduce_scalar(dramname, nblk, K, op, postmean=None):
            AR.mark()
            wf3d = [AR.alloc(4096, F32) for _ in range(2)]
            acc = AR.alloc(1, F32)
            red = AR.alloc(1, F32)
            for blk in range(nblk):
                wf3 = wf3d[blk % 2]
                wslab(dramname, blk, K, wf3)
                ve(lambda e, K=K, wf3=wf3: e.tensor_reduce(
                    out=red.ap[:, 0:1], in_=wf3.ap[:, 0:K], axis=AX.X,
                    op=op, apply_absolute_value=True),
                   ins=(wf3,), outs=(red,))
                if blk == 0:
                    ve(lambda e: e.tensor_copy(out=acc.ap[:, 0:1],
                                               in_=red.ap[:, 0:1]),
                       ins=(red,), outs=(acc,))
                else:
                    ve(lambda e, op=op: e.tensor_tensor(
                        out=acc.ap[:, 0:1], in0=acc.ap[:, 0:1],
                        in1=red.ap[:, 0:1], op=op),
                       ins=(red, acc), outs=(acc,))
            te(lambda e: e.transpose(out=PS[5].ap[0:1, 0:128],
                                     in_=acc.ap[:, 0:1],
                                     identity=ident_f.ap[:]),
               ins=(acc, ident_f), outs=(PS[5],))
            out_row = srow(1)
            if op == ALU.add:
                ve(lambda e: e.tensor_reduce(
                    out=out_row.ap[0:1, 0:1], in_=PS[5].ap[0:1, 0:128],
                    axis=AX.X, op=ALU.add), ins=(PS[5],),
                   outs=(out_row,))
                ve(lambda e, pm=postmean: e.tensor_scalar(
                    out=out_row.ap[0:1, 0:1], in0=out_row.ap[0:1, 0:1],
                    scalar1=pm, scalar2=1e-5, op0=ALU.mult, op1=ALU.max),
                   ins=(out_row,), outs=(out_row,))
            else:
                ve(lambda e: e.tensor_reduce(
                    out=out_row.ap[0:1, 0:1], in_=PS[5].ap[0:1, 0:128],
                    axis=AX.X, op=ALU.max), ins=(PS[5],),
                   outs=(out_row,))
            AR.release()
            return out_row

        def col_of(row_buf, scale=1.0, recip=False):
            tmpr = srow(1)
            if recip:
                ve(lambda e: e.reciprocal(out=tmpr.ap[0:1, 0:1],
                                          in_=row_buf.ap[0:1, 0:1]),
                   ins=(row_buf,), outs=(tmpr,))
            else:
                ve(lambda e: e.tensor_copy(out=tmpr.ap[0:1, 0:1],
                                           in_=row_buf.ap[0:1, 0:1]),
                   ins=(row_buf,), outs=(tmpr,))
            if scale != 1.0:
                ve(lambda e, s=scale: e.tensor_scalar(
                    out=tmpr.ap[0:1, 0:1], in0=tmpr.ap[0:1, 0:1],
                    scalar1=s, scalar2=None, op0=ALU.mult),
                   ins=(tmpr,), outs=(tmpr,))
            mm([ones_c.ap[0:1, :]], [tmpr.ap[0:1, 0:1]],
               PS[5].ap[:, 0:1], ins=(tmpr, ones_c), outs=(PS[5],))
            colb = CR.alloc(1, F32)
            se(lambda e: e.copy(out=colb.ap[:, 0:1],
                                in_=PS[5].ap[:, 0:1]),
               ins=(PS[5],), outs=(colb,))
            return colb

        mean_w1 = stream_reduce_scalar("w1T", 32, 1024, ALU.add,
                                       postmean=1.0 / (DFF * D))
        mean_w2 = stream_reduce_scalar("w2T", 8, 4096, ALU.add,
                                       postmean=1.0 / (DFF * D))
        s_w1c = col_of(mean_w1, recip=True)
        s_w2c = col_of(mean_w2, recip=True)

        def quant_stream(dramname, nblk, K, scol, emit_dst):
            AR.mark()
            wf3d = [AR.alloc(4096, F32) for _ in range(2)]
            qi = AR.alloc(4096, I32)
            qf = AR.alloc(4096, F32)
            qb = AR.alloc(4096, BF16)
            for blk in range(nblk):
                wf3 = wf3d[blk % 2]
                wslab(dramname, blk, K, wf3)
                ve(lambda e, K=K, wf3=wf3: e.tensor_scalar(
                    out=qf.ap[:, 0:K], in0=wf3.ap[:, 0:K],
                    scalar1=scol.ap[:, 0:1], scalar2=None, op0=ALU.mult),
                   ins=(wf3, scol), outs=(qf,))
                ve(lambda e, K=K: e.tensor_copy(out=qi.ap[:, 0:K],
                                                in_=qf.ap[:, 0:K]),
                   ins=(qf,), outs=(qi,))
                ve(lambda e, K=K: e.tensor_copy(out=qf.ap[:, 0:K],
                                                in_=qi.ap[:, 0:K]),
                   ins=(qi,), outs=(qf,))
                ve(lambda e, K=K: e.tensor_scalar(
                    out=qb.ap[:, 0:K], in0=qf.ap[:, 0:K], scalar1=-1.0,
                    scalar2=1.0, op0=ALU.max, op1=ALU.min),
                   ins=(qf,), outs=(qb,))
                emit_dst(blk, qb)
            AR.release()

        AR.mark()
        WQ1 = [AR.alloc(1024, BF16) for _ in range(NFT)]

        def w1_dst(blk, qb):
            ve(lambda e, blk=blk: e.tensor_copy(out=WQ1[blk].ap[:],
                                                in_=qb.ap[:, 0:1024]),
               ins=(qb,), outs=(WQ1[blk],))
        quant_stream("w1T", 32, 1024, s_w1c, w1_dst)

        QIN = [AR.alloc(T, BF16) for _ in range(NJ)]
        qin_h = CR.alloc(2 * NJ, BF16)
        srow_x, sirow_c, rs_x = RS[4], RS[2], RS[3]
        AR.mark()
        xmw = AR.alloc(NJ * TC, F32)
        xm = [Buf(xmw.ap[:, j * TC:(j + 1) * TC]) for j in range(NJ)]
        Mt = AR.alloc(TC, F32)
        xt2d = [AR.alloc(TC, F32) for _ in range(2)]
        for t in range(4):
            bcast(mu0, t * TC, TC, PS[6])
            for j in range(NJ):
                xt2 = xt2d[j % 2]
                dma("sy", xt2.ap[:],
                    din["x0"][j * 128:(j + 1) * 128, t * TC:(t + 1) * TC],
                    outs=(xt2,))
                ve(lambda e, j=j, xt2=xt2: e.tensor_tensor(
                    out=xm[j].ap[:], in0=xt2.ap[:], in1=PS[6].ap[:, 0:TC],
                    op=ALU.subtract), ins=(xt2, PS[6]), outs=(xm[j],))
            ve(lambda e: e.tensor_reduce(
                out=Mt.ap[:],
                in_=xmw.ap[:, 0:NJ * TC].rearrange("p (j t) -> p t j",
                                                   j=NJ),
                axis=AX.X, op=ALU.max, apply_absolute_value=True),
               ins=tuple(xm), outs=(Mt,))
            for k in range(0, TC, 128):
                w = min(128, TC - k)
                gt = t * TC + k
                te(lambda e, k=k, w=w: e.transpose(
                    out=PS[5].ap[0:w, 0:128],
                    in_=Mt.ap[:, k:k + w],
                    identity=ident_f.ap[:]), ins=(Mt, ident_f),
                   outs=(PS[5],))
                ve(lambda e, w=w, pb=gt % 128, c=gt // 128:
                   e.tensor_reduce(
                    out=mxc.ap[pb:pb + w, c:c + 1],
                    in_=PS[5].ap[0:w, 0:128], axis=AX.X, op=ALU.max,
                    apply_absolute_value=True), ins=(PS[5],),
                   outs=(mxc,))
        AR.release()
        te(lambda e: e.transpose(out=PS[5].ap[0:nkT, 0:128],
                                 in_=mxc.ap[:, 0:nkT],
                                 identity=ident_f.ap[:]),
           ins=(mxc, ident_f), outs=(PS[5],))
        se(lambda e: e.copy(out=mxsb.ap[0:nkT, 0:128],
                            in_=PS[5].ap[0:nkT, 0:128]),
           ins=(PS[5],), outs=(mxsb,))
        dma("se", srow_x.ap[0:1, 0:T], mxsb.ap[0:nkT, 0:128],
            ins=(mxsb,), outs=(srow_x,))

        def finish_scale_rows(sro, r_row, siro, width, meanrow):
            ve(lambda e, w=width: e.tensor_tensor(
                out=sro.ap[0:1, 0:w], in0=sro.ap[0:1, 0:w],
                in1=r_row.ap[0:1, 0:w], op=ALU.mult),
               ins=(sro, r_row), outs=(sro,))
            ve(lambda e, w=width: e.tensor_scalar(
                out=sro.ap[0:1, 0:w], in0=sro.ap[0:1, 0:w], scalar1=1e-5,
                scalar2=None, op0=ALU.max), ins=(sro,), outs=(sro,))
            ve(lambda e, w=width: e.tensor_scalar(
                out=siro.ap[0:1, 0:w], in0=sro.ap[0:1, 0:w],
                scalar1=1.0 / 127.0, scalar2=None, op0=ALU.mult),
               ins=(sro,), outs=(siro,))
            ve(lambda e, w=width: e.tensor_scalar(
                out=siro.ap[0:1, 0:w], in0=siro.ap[0:1, 0:w],
                scalar1=meanrow.ap[0:1, 0:1], scalar2=None,
                op0=ALU.mult), ins=(siro, meanrow), outs=(siro,))
            ve(lambda e, w=width: e.reciprocal(
                out=sro.ap[0:1, 0:w], in_=sro.ap[0:1, 0:w]),
               ins=(sro,), outs=(sro,))
            ve(lambda e, w=width: e.tensor_scalar(
                out=sro.ap[0:1, 0:w], in0=sro.ap[0:1, 0:w],
                scalar1=127.0, scalar2=None, op0=ALU.mult),
               ins=(sro,), outs=(sro,))

        finish_scale_rows(srow_x, r0, sirow_c, T, mean_w1)
        ve(lambda e: e.tensor_tensor(out=rs_x.ap[0:1, 0:T],
                                     in0=r0.ap[0:1, 0:T],
                                     in1=srow_x.ap[0:1, 0:T],
                                     op=ALU.mult),
           ins=(r0, srow_x), outs=(rs_x,))
        AR.mark()
        tmped = [AR.alloc(TC, F32) for _ in range(2)]
        qie = AR.alloc(TC, I32)
        for t in range(4):
            bcast(rs_x, t * TC, TC, PS[7])
            bcast(mu0, t * TC, TC, PS[6])
            for j in range(NJ):
                tmpe = tmped[j % 2]
                dma("sy", tmpe.ap[:],
                    din["x0"][j * 128:(j + 1) * 128, t * TC:(t + 1) * TC],
                    outs=(tmpe,))
                ve(lambda e, tmpe=tmpe: e.tensor_tensor(
                    out=tmpe.ap[:], in0=tmpe.ap[:],
                    in1=PS[6].ap[:, 0:TC], op=ALU.subtract),
                   ins=(tmpe, PS[6]), outs=(tmpe,))
                ve(lambda e, tmpe=tmpe: e.tensor_tensor(
                    out=tmpe.ap[:], in0=tmpe.ap[:],
                    in1=PS[7].ap[:, 0:TC], op=ALU.mult),
                   ins=(tmpe, PS[7]), outs=(tmpe,))
                ve(lambda e, tmpe=tmpe: e.tensor_copy(
                    out=qie.ap[:], in_=tmpe.ap[:]),
                   ins=(tmpe,), outs=(qie,))
                ve(lambda e, j=j, t=t: e.tensor_copy(
                    out=QIN[j].ap[:, t * TC:(t + 1) * TC], in_=qie.ap[:]),
                   ins=(qie,), outs=(QIN[j],))
        AR.release()
        dbg_dump("d_qin", QIN)
        hsr = srow(2)
        hsi = srow(2)
        hrs = srow(2)
        xmh = CR.alloc(2 * NJ, F32)
        Mh = CR.alloc(4, F32)
        hcol2 = CR.alloc(2, F32)
        bcast(mu0h, 0, 2, PS[6])
        for j in range(NJ):
            ve(lambda e, j=j: e.tensor_tensor(
                out=xmh.ap[:, 2 * j:2 * j + 2],
                in0=x0h_t.ap[:, 2 * j:2 * j + 2], in1=PS[6].ap[:, 0:2],
                op=ALU.subtract), ins=(x0h_t, PS[6]), outs=(xmh,))
        ve(lambda e: e.tensor_reduce(
            out=Mh.ap[:, 0:2],
            in_=xmh.ap[:, 0:2 * NJ].rearrange("p (j i) -> p i j", j=NJ),
            axis=AX.X, op=ALU.max, apply_absolute_value=True),
           ins=(xmh,), outs=(Mh,))
        te(lambda e: e.transpose(out=PS[5].ap[0:2, 0:128],
                                 in_=Mh.ap[:, 0:2],
                                 identity=ident_f.ap[:]),
           ins=(Mh, ident_f), outs=(PS[5],))
        ve(lambda e: e.tensor_reduce(out=hcol2.ap[0:2, 0:1],
                                     in_=PS[5].ap[0:2, 0:128], axis=AX.X,
                                     op=ALU.max,
                                     apply_absolute_value=True),
           ins=(PS[5],), outs=(hcol2,))
        te(lambda e: e.transpose(out=PS[5].ap[0:1, 0:2],
                                 in_=hcol2.ap[0:2, 0:1],
                                 identity=ident_f.ap[0:2, 0:2]),
           ins=(hcol2, ident_f), outs=(PS[5],))
        se(lambda e: e.copy(out=hsr.ap[0:1, 0:2],
                            in_=PS[5].ap[0:1, 0:2]),
           ins=(PS[5],), outs=(hsr,))
        finish_scale_rows(hsr, r0h, hsi, 2, mean_w1)
        ve(lambda e: e.tensor_tensor(out=hrs.ap[0:1, 0:2],
                                     in0=r0h.ap[0:1, 0:2],
                                     in1=hsr.ap[0:1, 0:2], op=ALU.mult),
           ins=(r0h, hsr), outs=(hrs,))
        bcast(hrs, 0, 2, PS[7])
        AR.mark()
        tmph = AR.alloc(2, F32)
        qih = AR.alloc(2, I32)
        for j in range(NJ):
            ve(lambda e, j=j: e.tensor_tensor(
                out=tmph.ap[:, 0:2], in0=xmh.ap[:, 2 * j:2 * j + 2],
                in1=PS[7].ap[:, 0:2], op=ALU.mult),
               ins=(xmh, PS[7]), outs=(tmph,))
            ve(lambda e: e.tensor_copy(out=qih.ap[:, 0:2],
                                       in_=tmph.ap[:, 0:2]),
               ins=(tmph,), outs=(qih,))
            ve(lambda e, j=j: e.tensor_copy(
                out=qin_h.ap[:, 2 * j:2 * j + 2], in_=qih.ap[:, 0:2]),
               ins=(qih,), outs=(qin_h,))
        AR.release()
        AR.mark()
        hb = AR.alloc(T, F32)
        hbh = AR.alloc(2, F32)
        acc = AR.alloc(T, F32)
        hcb = AR.alloc(T, BF16)
        sxc = AR.alloc(T, F32)
        hsib = AR.alloc(2, F32)
        for t in range(4):
            bcast(sirow_c, t * TC, TC, PS[6])
            se(lambda e, t=t: e.copy(out=sxc.ap[:, t * TC:(t + 1) * TC],
                                     in_=PS[6].ap[:, 0:TC]),
               ins=(PS[6],), outs=(sxc,))
        bcast(hsi, 0, 2, PS[6])
        se(lambda e: e.copy(out=hsib.ap[:, 0:2], in_=PS[6].ap[:, 0:2]),
           ins=(PS[6],), outs=(hsib,))
        for ot in range(NFT):
            for t in range(4):
                mm([WQ1[ot].ap[:, kc * 128:(kc + 1) * 128]
                    for kc in range(8)],
                   [QIN[kc].ap[:, t * TC:(t + 1) * TC]
                    for kc in range(8)],
                   PS[t % 4].ap[:, 0:TC], ins=[WQ1[ot]] + QIN,
                   outs=(PS[t % 4],))
                ve(lambda e, t=t, ps=PS[t % 4]: e.tensor_tensor(
                    out=hb.ap[:, t * TC:(t + 1) * TC],
                    in0=ps.ap[:, 0:TC], in1=sxc.ap[:, t * TC:(t + 1) * TC],
                    op=ALU.mult), ins=(PS[t % 4], sxc), outs=(hb,))
            mm([WQ1[ot].ap[:, kc * 128:(kc + 1) * 128]
                for kc in range(8)],
               [qin_h.ap[:, 2 * kc:2 * kc + 2] for kc in range(8)],
               PS[4].ap[:, 0:2], ins=[WQ1[ot], qin_h], outs=(PS[4],))
            ve(lambda e: e.tensor_tensor(out=hbh.ap[:, 0:2],
                                         in0=PS[4].ap[:, 0:2],
                                         in1=hsib.ap[:, 0:2],
                                         op=ALU.mult),
               ins=(PS[4], hsib), outs=(hbh,))
            ve(lambda e, ot=ot: e.tensor_scalar(
                out=acc.ap[:], in0=hb.ap[:],
                scalar1=cols_t["cw1"].ap[:, ot:ot + 1], scalar2=None,
                op0=ALU.mult), ins=(hb, cols_t["cw1"]), outs=(acc,))
            ve(lambda e, ot=ot: e.scalar_tensor_tensor(
                out=acc.ap[:, 1:T], in0=hb.ap[:, 0:T - 1],
                scalar=cols_t["cw0"].ap[:, ot:ot + 1],
                in1=acc.ap[:, 1:T], op0=ALU.mult, op1=ALU.add),
               ins=(hb, cols_t["cw0"], acc), outs=(acc,))
            ve(lambda e, ot=ot: e.scalar_tensor_tensor(
                out=acc.ap[:, 0:1], in0=hbh.ap[:, 0:1],
                scalar=cols_t["cw0"].ap[:, ot:ot + 1],
                in1=acc.ap[:, 0:1], op0=ALU.mult, op1=ALU.add),
               ins=(hbh, cols_t["cw0"], acc), outs=(acc,))
            ve(lambda e, ot=ot: e.scalar_tensor_tensor(
                out=acc.ap[:, 0:T - 1], in0=hb.ap[:, 1:T],
                scalar=cols_t["cw2"].ap[:, ot:ot + 1],
                in1=acc.ap[:, 0:T - 1], op0=ALU.mult, op1=ALU.add),
               ins=(hb, cols_t["cw2"], acc), outs=(acc,))
            ve(lambda e, ot=ot: e.scalar_tensor_tensor(
                out=acc.ap[:, T - 1:T], in0=hbh.ap[:, 1:2],
                scalar=cols_t["cw2"].ap[:, ot:ot + 1],
                in1=acc.ap[:, T - 1:T], op0=ALU.mult, op1=ALU.add),
               ins=(hbh, cols_t["cw2"], acc), outs=(acc,))
            se(lambda e, ot=ot: e.activation(
                out=hcb.ap[:], in_=acc.ap[:], func=AF.Silu,
                bias=cols_t["cb"].ap[:, ot:ot + 1]),
               ins=(acc, cols_t["cb"]), outs=(hcb,))
            dma("se", sc_hc.ap[ot], hcb.ap[:], ins=(hcb,), outs=(sc_hc,))
            if DBG and "d_h" in dbg_d:
                dma("se", dbg_d["d_h"][ot * 128:(ot + 1) * 128, :],
                    hb.ap[:], ins=(hb,))
            if DBG and "d_hc" in dbg_d:
                AR.mark()
                tq = AR.alloc(T, F32)
                ve(lambda e, tq=tq, hcb=hcb: e.tensor_copy(out=tq.ap[:], in_=hcb.ap[:]),
                   ins=(hcb,), outs=(tq,))
                dma("se", dbg_d["d_hc"][ot * 128:(ot + 1) * 128, :],
                    tq.ap[:], ins=(tq,))
                AR.release()
        AR.release()
        AR.release()

        srow2, sirow2 = RS[4], RS[2]
        AR.mark()
        AR.mark()
        hcw = AR.alloc(NFT * TC, BF16)
        hcs = [Buf(hcw.ap[:, o * TC:(o + 1) * TC]) for o in range(NFT)]
        Mt2 = AR.alloc(TC, F32)
        for t in range(4):
            for ot in range(NFT):
                dma("sy", hcs[ot].ap[:],
                    sc_hc.ap[ot, :, t * TC:(t + 1) * TC],
                    ins=(sc_hc,), outs=(hcs[ot],))
            ve(lambda e: e.tensor_reduce(
                out=Mt2.ap[:],
                in_=hcw.ap[:, 0:NFT * TC].rearrange("p (o t) -> p t o",
                                                    o=NFT),
                axis=AX.X, op=ALU.max, apply_absolute_value=True),
               ins=tuple(hcs), outs=(Mt2,))
            for k in range(0, TC, 128):
                w = min(128, TC - k)
                gt = t * TC + k
                te(lambda e, k=k, w=w: e.transpose(
                    out=PS[5].ap[0:w, 0:128],
                    in_=Mt2.ap[:, k:k + w],
                    identity=ident_f.ap[:]), ins=(Mt2, ident_f),
                   outs=(PS[5],))
                ve(lambda e, w=w, pb=gt % 128, c=gt // 128:
                   e.tensor_reduce(
                    out=mxc.ap[pb:pb + w, c:c + 1],
                    in_=PS[5].ap[0:w, 0:128], axis=AX.X, op=ALU.max,
                    apply_absolute_value=True), ins=(PS[5],),
                   outs=(mxc,))
        AR.release()
        te(lambda e: e.transpose(out=PS[5].ap[0:nkT, 0:128],
                                 in_=mxc.ap[:, 0:nkT],
                                 identity=ident_f.ap[:]),
           ins=(mxc, ident_f), outs=(PS[5],))
        se(lambda e: e.copy(out=mxsb.ap[0:nkT, 0:128],
                            in_=PS[5].ap[0:nkT, 0:128]),
           ins=(PS[5],), outs=(mxsb,))
        dma("se", srow2.ap[0:1, 0:T], mxsb.ap[0:nkT, 0:128],
            ins=(mxsb,), outs=(srow2,))
        ve(lambda e: e.tensor_scalar(out=srow2.ap[0:1, 0:T],
                                     in0=srow2.ap[0:1, 0:T],
                                     scalar1=1e-5, scalar2=None,
                                     op0=ALU.max),
           ins=(srow2,), outs=(srow2,))
        ve(lambda e: e.tensor_scalar(out=sirow2.ap[0:1, 0:T],
                                     in0=srow2.ap[0:1, 0:T],
                                     scalar1=1.0 / 127.0, scalar2=None,
                                     op0=ALU.mult), ins=(srow2,),
           outs=(sirow2,))
        ve(lambda e: e.tensor_scalar(out=sirow2.ap[0:1, 0:T],
                                     in0=sirow2.ap[0:1, 0:T],
                                     scalar1=mean_w2.ap[0:1, 0:1],
                                     scalar2=None, op0=ALU.mult),
           ins=(sirow2, mean_w2), outs=(sirow2,))
        ve(lambda e: e.reciprocal(out=srow2.ap[0:1, 0:T],
                                  in_=srow2.ap[0:1, 0:T]),
           ins=(srow2,), outs=(srow2,))
        ve(lambda e: e.tensor_scalar(out=srow2.ap[0:1, 0:T],
                                     in0=srow2.ap[0:1, 0:T],
                                     scalar1=127.0, scalar2=None,
                                     op0=ALU.mult),
           ins=(srow2,), outs=(srow2,))
        WQ2 = [AR.alloc(1024, BF16) for _ in range(NFT)]

        def w2_dst(blk, qb):
            for kc in range(NFT):
                ve(lambda e, kc=kc, blk=blk: e.tensor_copy(
                    out=WQ2[kc].ap[:, blk * 128:(blk + 1) * 128],
                    in_=qb.ap[:, kc * 128:(kc + 1) * 128]),
                   ins=(qb,), outs=(WQ2[kc],))
        quant_stream("w2T", 8, 4096, s_w2c, w2_dst)
        AR.mark()
        hq = [AR.alloc(TC, BF16) for _ in range(NFT)]
        s0td = [AR.alloc(TC, F32) for _ in range(2)]
        hctd = [AR.alloc(TC, BF16) for _ in range(2)]
        tmp2 = AR.alloc(TC, F32)
        qi2 = AR.alloc(TC, I32)
        xt3d = [AR.alloc(TC, F32) for _ in range(2)]
        g0p = AR.alloc(TC, F32)
        sir2sb = AR.alloc(TC, F32)
        for t in range(4):
            bcast(srow2, t * TC, TC, PS[6])
            for ot in range(NFT):
                hct = hctd[ot % 2]
                dma("sy", hct.ap[:],
                    sc_hc.ap[ot, :, t * TC:(t + 1) * TC],
                    ins=(sc_hc,), outs=(hct,))
                ve(lambda e, hct=hct: e.tensor_tensor(
                    out=tmp2.ap[:], in0=hct.ap[:],
                    in1=PS[6].ap[:, 0:TC], op=ALU.mult),
                   ins=(hct, PS[6]), outs=(tmp2,))
                ve(lambda e: e.tensor_copy(out=qi2.ap[:],
                                           in_=tmp2.ap[:]),
                   ins=(tmp2,), outs=(qi2,))
                ve(lambda e, ot=ot: e.tensor_copy(out=hq[ot].ap[:],
                                                  in_=qi2.ap[:]),
                   ins=(qi2,), outs=(hq[ot],))
            bcast(sirow2, t * TC, TC, PS[6])
            se(lambda e: e.copy(out=sir2sb.ap[:, 0:TC],
                                in_=PS[6].ap[:, 0:TC]),
               ins=(PS[6],), outs=(sir2sb,))
            for j in range(NJ):
                s0t = s0td[j % 2]
                xt3 = xt3d[j % 2]
                mm([WQ2[kc].ap[:, j * 128:(j + 1) * 128]
                    for kc in range(NFT)],
                   [hq[kc].ap[:] for kc in range(NFT)],
                   PS[j % 4].ap[:, 0:TC], ins=WQ2 + hq,
                   outs=(PS[j % 4],))
                dma("sy", s0t.ap[:], sc_s0.ap[j, :, t * TC:(t + 1) * TC],
                    ins=(sc_s0,), outs=(s0t,))
                dma("sy", xt3.ap[:],
                    din["x0"][j * 128:(j + 1) * 128, t * TC:(t + 1) * TC],
                    outs=(xt3,))
                ve(lambda e, ps=PS[j % 4]: e.tensor_tensor(
                    out=g0p.ap[:], in0=ps.ap[:, 0:TC],
                    in1=sir2sb.ap[:, 0:TC], op=ALU.mult),
                   ins=(PS[j % 4], sir2sb), outs=(g0p,))
                se(lambda e: e.activation(out=g0p.ap[:], in_=g0p.ap[:],
                                          func=AF.Silu),
                   ins=(g0p,), outs=(g0p,))
                if DBG and "d_g0" in dbg_d:
                    dma("se", dbg_d["d_g0"][j * 128:(j + 1) * 128,
                                            t * TC:(t + 1) * TC],
                        g0p.ap[:], ins=(g0p,))
                ve(lambda e, s0t=s0t: e.tensor_tensor(
                    out=g0p.ap[:], in0=g0p.ap[:], in1=s0t.ap[:],
                    op=ALU.mult), ins=(g0p, s0t), outs=(g0p,))
                ve(lambda e, xt3=xt3: e.tensor_tensor(
                    out=xt3.ap[:], in0=xt3.ap[:], in1=g0p.ap[:],
                    op=ALU.add), ins=(g0p, xt3), outs=(xt3,))
                dma("se", sc_x1.ap[j, :, t * TC:(t + 1) * TC], xt3.ap[:],
                    ins=(xt3,), outs=(sc_x1,))
                if t == 3:
                    dma("se", agh_in.ap[j, :, 0:3], xt3.ap[:, TC - 3:TC],
                        ins=(xt3,), outs=(agh_in,))
                if DBG and "d_x1" in dbg_d:
                    dma("se", dbg_d["d_x1"][j * 128:(j + 1) * 128,
                                            t * TC:(t + 1) * TC],
                        xt3.ap[:], ins=(xt3,))
                del s0t, xt3
        AR.release()
        AR.release()
        cc("AllGather", agh_in, agh_out)

        AR.mark()
        G1 = [AR.alloc(T, BF16) for _ in range(NJ)]
        AR.mark()
        XN1 = [AR.alloc(T, BF16) for _ in range(NJ)]
        mu3, r3 = RS[3], RS[4]
        AR.mark()
        xf2 = AR.alloc(512, F32)

        def fill_x1(j, lo, w, xb):
            dma("sy", xf2.ap[:, 0:w], sc_x1.ap[j, :, lo:lo + w],
                ins=(sc_x1,), outs=(xf2,))
            ve(lambda e, w=w: e.tensor_copy(out=xb.ap[:, 0:w],
                                            in_=xf2.ap[:, 0:w]),
               ins=(xf2,), outs=(xb,))
        ln_stats(fill_x1, NJ, T, mu3, r3)
        tmp4 = AR.alloc(TC, F32)
        for t in range(4):
            bcast(mu3, t * TC, TC, PS[6])
            bcast(r3, t * TC, TC, PS[7])
            for j in range(NJ):
                dma("sy", tmp4.ap[:], sc_x1.ap[j, :, t * TC:(t + 1) * TC],
                    ins=(sc_x1,), outs=(tmp4,))
                ve(lambda e: e.tensor_tensor(out=tmp4.ap[:],
                                             in0=tmp4.ap[:],
                                             in1=PS[6].ap[:, 0:TC],
                                             op=ALU.subtract),
                   ins=(tmp4, PS[6]), outs=(tmp4,))
                ve(lambda e, j=j, t=t: e.tensor_tensor(
                    out=XN1[j].ap[:, t * TC:(t + 1) * TC],
                    in0=tmp4.ap[:], in1=PS[7].ap[:, 0:TC], op=ALU.mult),
                   ins=(tmp4, PS[7]), outs=(XN1[j],))
        AR.release()
        x1h = CR.alloc(3 * NJ, F32)
        xn1h = CR.alloc(3 * NJ, BF16)
        muh = srow(3)
        rh = srow(3)
        AR.mark()
        slots = AR.alloc(12, F32)
        acch = AR.alloc(3, F32)
        for j in range(NJ):
            for r in range(4):
                dma("sy", slots.ap[:, 3 * r:3 * r + 3], agh_out.ap[r, j],
                    ins=(agh_out,), outs=(slots,))
            ve(lambda e: e.tensor_scalar(
                out=acch.ap[:, 0:3], in0=slots.ap[:, 0:3],
                scalar1=halo_m.ap[:, 0:1], scalar2=None, op0=ALU.mult),
               ins=(slots, halo_m), outs=(acch,))
            for r in range(1, 4):
                ve(lambda e, r=r: e.scalar_tensor_tensor(
                    out=acch.ap[:, 0:3],
                    in0=slots.ap[:, 3 * r:3 * r + 3],
                    scalar=halo_m.ap[:, r:r + 1], in1=acch.ap[:, 0:3],
                    op0=ALU.mult, op1=ALU.add),
                   ins=(slots, halo_m, acch), outs=(acch,))
            ve(lambda e, j=j: e.tensor_copy(
                out=x1h.ap[:, 3 * j:3 * j + 3], in_=acch.ap[:, 0:3]),
               ins=(acch,), outs=(x1h,))
        ln_stats(fill_from_cr(x1h, percol=3), NJ, 3, muh, rh)
        bcast(muh, 0, 3, PS[6])
        bcast(rh, 0, 3, PS[7])
        tmp3 = AR.alloc(3, F32)
        for j in range(NJ):
            ve(lambda e, j=j: e.tensor_tensor(
                out=tmp3.ap[:, 0:3], in0=x1h.ap[:, 3 * j:3 * j + 3],
                in1=PS[6].ap[:, 0:3], op=ALU.subtract),
               ins=(x1h, PS[6]), outs=(tmp3,))
            ve(lambda e, j=j: e.tensor_tensor(
                out=xn1h.ap[:, 3 * j:3 * j + 3], in0=tmp3.ap[:, 0:3],
                in1=PS[7].ap[:, 0:3], op=ALU.mult),
               ins=(tmp3, PS[7]), outs=(xn1h,))
        AR.release()

        AR.mark()
        UCB = [AR.alloc(T, BF16) for _ in range(NOT)]
        AR.mark()
        sla = AR.alloc(1024, BF16)
        uf = AR.alloc(T, F32)
        uh = AR.alloc(3, F32)
        accu = AR.alloc(T, F32)
        zbb = AR.alloc(T, BF16)
        for ot in range(NFT):
            wslab("W_in", ot, 1024, sla)
            if ot < NOT:
                for t in range(4):
                    mm([sla.ap[:, kc * 128:(kc + 1) * 128]
                        for kc in range(8)],
                       [XN1[kc].ap[:, t * TC:(t + 1) * TC]
                        for kc in range(8)],
                       PS[t % 4].ap[:, 0:TC], ins=[sla] + XN1,
                       outs=(PS[t % 4],))
                    se(lambda e, t=t, ps=PS[t % 4]: e.copy(
                        out=uf.ap[:, t * TC:(t + 1) * TC],
                        in_=ps.ap[:, 0:TC]), ins=(PS[t % 4],), outs=(uf,))
                mm([sla.ap[:, kc * 128:(kc + 1) * 128]
                    for kc in range(8)],
                   [xn1h.ap[:, 3 * kc:3 * kc + 3] for kc in range(8)],
                   PS[4].ap[:, 0:3], ins=[sla, xn1h], outs=(PS[4],))
                se(lambda e: e.copy(out=uh.ap[:, 0:3],
                                    in_=PS[4].ap[:, 0:3]),
                   ins=(PS[4],), outs=(uh,))
                ve(lambda e, ot=ot: e.tensor_scalar(
                    out=accu.ap[:], in0=uf.ap[:],
                    scalar1=cols_t["c1w3"].ap[:, ot:ot + 1],
                    scalar2=None, op0=ALU.mult),
                   ins=(uf, cols_t["c1w3"]), outs=(accu,))
                for sh, cw in [(3, "c1w0"), (2, "c1w1"), (1, "c1w2")]:
                    ve(lambda e, ot=ot, sh=sh, cw=cw:
                       e.scalar_tensor_tensor(
                           out=accu.ap[:, sh:T], in0=uf.ap[:, 0:T - sh],
                           scalar=cols_t[cw].ap[:, ot:ot + 1],
                           in1=accu.ap[:, sh:T], op0=ALU.mult,
                           op1=ALU.add),
                       ins=(uf, cols_t[cw], accu), outs=(accu,))
                    ve(lambda e, ot=ot, sh=sh, cw=cw:
                       e.scalar_tensor_tensor(
                           out=accu.ap[:, 0:sh],
                           in0=uh.ap[:, 3 - sh:3],
                           scalar=cols_t[cw].ap[:, ot:ot + 1],
                           in1=accu.ap[:, 0:sh], op0=ALU.mult,
                           op1=ALU.add),
                       ins=(uh, cols_t[cw], accu), outs=(accu,))
                se(lambda e, ot=ot: e.activation(
                    out=UCB[ot].ap[:], in_=accu.ap[:], func=AF.Silu,
                    bias=cols_t["c1b"].ap[:, ot:ot + 1]),
                   ins=(accu, cols_t["c1b"]), outs=(UCB[ot],))
            else:
                oz = ot - NOT
                for t in range(4):
                    mm([sla.ap[:, kc * 128:(kc + 1) * 128]
                        for kc in range(8)],
                       [XN1[kc].ap[:, t * TC:(t + 1) * TC]
                        for kc in range(8)],
                       PS[t % 4].ap[:, 0:TC], ins=[sla] + XN1,
                       outs=(PS[t % 4],))
                    se(lambda e, t=t, ps=PS[t % 4]: e.copy(
                        out=zbb.ap[:, t * TC:(t + 1) * TC],
                        in_=ps.ap[:, 0:TC]), ins=(PS[t % 4],), outs=(zbb,))
                dma("se", sc_z.ap[oz], zbb.ap[:], ins=(zbb,),
                    outs=(sc_z,))
        AR.release()
        dbg_dump("d_uc", UCB)

        AR.mark()
        dtrb = AR.alloc(T, BF16)
        negbc = AR.alloc(T, BF16)
        AR.mark()
        wx = AR.alloc(16 * 160, BF16)
        for kc in range(16):
            dma("sy", wx.ap[:, kc * 160:(kc + 1) * 160],
                din["W_x"][kc * 128:(kc + 1) * 128, :], outs=(wx,))
        for t in range(4):
            mm([wx.ap[:, kc * 160:kc * 160 + 128] for kc in range(16)],
               [UCB[kc].ap[:, t * TC:(t + 1) * TC] for kc in range(16)],
               PS[0].ap[:, 0:TC], ins=[wx] + UCB, outs=(PS[0],))
            se(lambda e, t=t: e.copy(out=dtrb.ap[:, t * TC:(t + 1) * TC],
                                     in_=PS[0].ap[:, 0:TC]),
               ins=(PS[0],), outs=(dtrb,))
            mm([wx.ap[:, kc * 160 + 128:(kc + 1) * 160]
                for kc in range(16)],
               [UCB[kc].ap[:, t * TC:(t + 1) * TC] for kc in range(16)],
               PS[1].ap[0:32, 0:TC], ins=[wx] + UCB, outs=(PS[1],))
            ve(lambda e, t=t: e.tensor_scalar(
                out=negbc.ap[0:32, t * TC:(t + 1) * TC],
                in0=PS[1].ap[0:32, 0:TC],
                scalar1=sgnbc.ap[0:32, 0:1], scalar2=None,
                op0=ALU.mult), ins=(PS[1], sgnbc), outs=(negbc,))
        dma("se", agbc_in.ap[:], negbc.ap[0:32, :], ins=(negbc,),
            outs=(agbc_in,))
        cc("AllGather", agbc_in, agbc_out, rg=RG8)
        if DBG and "d_bc" in dbg_d:
            AR.mark()
            tq = AR.alloc(T, F32)
            ve(lambda e, tq=tq: e.tensor_copy(out=tq.ap[0:32, 0:T],
                                       in_=negbc.ap[0:32, :]),
               ins=(negbc,), outs=(tq,))
            dma("se", dbg_d["d_bc"], tq.ap[0:32, 0:T], ins=(tq,))
            AR.release()

        AR.release()
        wdt = AR.alloc(DI, BF16)
        dma("sy",
            wdt.ap[:, 0:DI].rearrange("k (nb c) -> k nb c", nb=16),
            din["W_dt"].rearrange("nb k c -> k nb c"), outs=(wdt,))
        AR.mark()
        dtf = AR.alloc(T, BF16)
        axx = AR.alloc(TC, F32)
        exx = AR.alloc(TC, F32)
        rxx = AR.alloc(TC, F32)
        for ot in range(NOT):
            for t in range(4):
                mm([wdt.ap[:, ot * 128:(ot + 1) * 128]],
                   [dtrb.ap[:, t * TC:(t + 1) * TC]],
                   PS[t % 2].ap[:, 0:TC], ins=(wdt, dtrb),
                   outs=(PS[t % 2],))
                se(lambda e, ot=ot, ps=PS[t % 2]: e.activation(
                    out=axx.ap[:], in_=ps.ap[:, 0:TC], func=AF.Abs,
                    bias=cols_t["bdt"].ap[:, ot:ot + 1]),
                   ins=(PS[t % 2], cols_t["bdt"]), outs=(axx,))
                se(lambda e, ot=ot, ps=PS[t % 2]: e.activation(
                    out=rxx.ap[:], in_=ps.ap[:, 0:TC], func=AF.Relu,
                    bias=cols_t["bdt"].ap[:, ot:ot + 1]),
                   ins=(PS[t % 2], cols_t["bdt"]), outs=(rxx,))
                se(lambda e: e.activation(out=exx.ap[:], in_=axx.ap[:],
                                          func=AF.Exp, scale=-1.0),
                   ins=(axx,), outs=(exx,))
                ve(lambda e: e.tensor_scalar(out=exx.ap[:],
                                             in0=exx.ap[:],
                                             scalar1=1.0, scalar2=None,
                                             op0=ALU.add),
                   ins=(exx,), outs=(exx,))
                se(lambda e: e.activation(out=exx.ap[:], in_=exx.ap[:],
                                          func=AF.Ln), ins=(exx,),
                   outs=(exx,))
                ve(lambda e, t=t: e.tensor_tensor(
                    out=dtf.ap[:, t * TC:(t + 1) * TC], in0=exx.ap[:],
                    in1=rxx.ap[:], op=ALU.add), ins=(exx, rxx),
                   outs=(dtf,))
            sl_, jj = ot // 2, ot % 2
            dma("se", a1_in.ap[sl_, 0, jj], dtf.ap[:], ins=(dtf,),
                outs=(a1_in,))
            dma("se", a1_in.ap[sl_, 1, jj], UCB[ot].ap[:],
                ins=(UCB[ot],), outs=(a1_in,))
            if DBG and "d_dt" in dbg_d:
                AR.mark()
                tq = AR.alloc(T, F32)
                ve(lambda e, tq=tq, dtf=dtf: e.tensor_copy(out=tq.ap[:], in_=dtf.ap[:]),
                   ins=(dtf,), outs=(tq,))
                dma("se", dbg_d["d_dt"][ot * 128:(ot + 1) * 128, :],
                    tq.ap[:], ins=(tq,))
                AR.release()
        AR.release()
        AR.release()
        AR.release()
        cc("AllToAll", a1_in, a1_out, rg=RG8)

        kmax = stream_reduce_scalar("Kf", NJ, 1024, ALU.max)
        kscol = col_of(kmax, scale=1.0 / 7.0)
        ksrcol = col_of(kmax, recip=True)
        ve(lambda e: e.tensor_scalar(out=ksrcol.ap[:, 0:1],
                                     in0=ksrcol.ap[:, 0:1], scalar1=7.0,
                                     scalar2=None, op0=ALU.mult),
           ins=(ksrcol,), outs=(ksrcol,))
        AR.mark()
        KQ = [AR.alloc(1024, BF16) for _ in range(NJ)]
        wf3 = AR.alloc(1024, F32)
        qi3 = AR.alloc(1024, I32)
        for blk in range(NJ):
            wslab("Kf", blk, 1024, wf3)
            ve(lambda e: e.tensor_scalar(out=wf3.ap[:, 0:1024],
                                         in0=wf3.ap[:, 0:1024],
                                         scalar1=ksrcol.ap[:, 0:1],
                                         scalar2=None, op0=ALU.mult),
               ins=(wf3, ksrcol), outs=(wf3,))
            ve(lambda e: e.tensor_copy(out=qi3.ap[:],
                                       in_=wf3.ap[:, 0:1024]),
               ins=(wf3,), outs=(qi3,))
            ve(lambda e, blk=blk: e.tensor_copy(out=KQ[blk].ap[:],
                                                in_=qi3.ap[:]),
               ins=(qi3,), outs=(KQ[blk],))
        alo = AR.alloc(8 * 16, BF16)
        blo = AR.alloc(D, BF16)
        blor = AR.alloc(D, BF16)
        for kc in range(8):
            dma("sy", alo.ap[:, kc * 16:(kc + 1) * 16],
                din["a_lora"][kc * 128:(kc + 1) * 128, :], outs=(alo,))
        dma("sy", blor.ap[0:16, :], din["b_lora"], outs=(blor,))
        ve(lambda e: e.tensor_scalar(out=blo.ap[0:16, :],
                                     in0=blor.ap[0:16, :],
                                     scalar1=ksrcol.ap[0:16, 0:1],
                                     scalar2=None, op0=ALU.mult),
           ins=(blor, ksrcol), outs=(blo,))
        r1b = AR.alloc(TC, BF16)
        for t in range(4):
            mm([alo.ap[:, kc * 16:(kc + 1) * 16] for kc in range(8)],
               [XN1[kc].ap[:, t * TC:(t + 1) * TC] for kc in range(8)],
               PS[4].ap[0:16, 0:TC], ins=[alo] + XN1, outs=(PS[4],))
            ve(lambda e: e.tensor_copy(out=r1b.ap[0:16, :],
                                       in_=PS[4].ap[0:16, 0:TC]),
               ins=(PS[4],), outs=(r1b,))
            for j in range(NJ):
                mm([KQ[j].ap[:, kc * 128:(kc + 1) * 128]
                    for kc in range(8)],
                   [XN1[kc].ap[:, t * TC:(t + 1) * TC]
                    for kc in range(8)],
                   PS[j % 4].ap[:, 0:TC], ins=KQ + XN1,
                   outs=(PS[j % 4],), final=False)
                mm([blo.ap[0:16, j * 128:(j + 1) * 128]],
                   [r1b.ap[0:16, :]],
                   PS[j % 4].ap[:, 0:TC], ins=(blo, r1b),
                   outs=(PS[j % 4],), accum=True)
                se(lambda e, j=j, t=t, ps=PS[j % 4]: e.activation(
                    out=G1[j].ap[:, t * TC:(t + 1) * TC],
                    in_=ps.ap[:, 0:TC], func=AF.Silu,
                    scale=kscol.ap[:, 0:1],
                    bias=cols_t["bias1f"].ap[:, j:j + 1]),
                   ins=(PS[j % 4], kscol, cols_t["bias1f"]), outs=(G1[j],))
        AR.release()
        dbg_dump("d_g1", G1)
        AR.release()

        AR.mark()
        CWS = min(1024, T)
        NCH = L // CWS
        stc2 = AR.alloc(2 * NST, F32)
        dts2 = [[AR.alloc(CWS, BF16) for _ in range(2)] for _ in range(2)]
        ucs2 = [[AR.alloc(CWS, BF16) for _ in range(2)] for _ in range(2)]
        dtu2 = [[AR.alloc(CWS, F32) for _ in range(2)] for _ in range(2)]
        dA2 = [[AR.alloc(CWS, F32) for _ in range(2)] for _ in range(2)]
        dBu2 = [AR.alloc(CWS, F32) for _ in range(2)]
        hh2 = [[AR.alloc(CWS, F32) for _ in range(2)] for _ in range(2)]
        Bsb = [AR.alloc(CWS, BF16) for _ in range(2)]
        Csb = [AR.alloc(CWS, BF16) for _ in range(2)]
        ytp = [AR.alloc(CWS, F32) for _ in range(2)]
        ycc = [AR.alloc(CWS, F32) for _ in range(2)]
        ybc2 = [AR.alloc(CWS, BF16) for _ in range(2)]
        for b_ in range(2):
            ve(lambda e: e.memset(stc2.ap[:], 0.0), outs=(stc2,))
            for ch in range(NCH):
                par = ch % 2
                off = (ch * CWS) % T
                sl0 = 4 * b_ + (ch * CWS) // T
                for oidx in range(2):
                    dma("ge", dts2[oidx][par].ap[:],
                        a1_out.ap[sl0, 0, oidx][:, off:off + CWS],
                        ins=(a1_out,), outs=(dts2[oidx][par],))
                    dma("ge", ucs2[oidx][par].ap[:],
                        a1_out.ap[sl0, 1, oidx][:, off:off + CWS],
                        ins=(a1_out,), outs=(ucs2[oidx][par],))
                    ve(lambda e, o=oidx, p=par: e.tensor_tensor(
                        out=dtu2[o][p].ap[:], in0=dts2[o][p].ap[:],
                        in1=ucs2[o][p].ap[:], op=ALU.mult),
                       ins=(dts2[oidx][par], ucs2[oidx][par]),
                       outs=(dtu2[oidx][par],))
                for n in range(NST):
                    npar = n % 2
                    dma("sy", Bsb[npar].ap[:],
                        agbc_out.ap[sl0, n:n + 1, off:off + CWS]
                        .broadcast_to((128, CWS)),
                        ins=(agbc_out,), outs=(Bsb[npar],))
                    dma("se", Csb[npar].ap[:],
                        agbc_out.ap[sl0, 16 + n:17 + n, off:off + CWS]
                        .broadcast_to((128, CWS)),
                        ins=(agbc_out,), outs=(Csb[npar],))
                    for oidx in range(2):
                        se(lambda e, o=oidx, p=par, np_=npar, n=n:
                           e.activation(
                            out=dA2[o][np_].ap[:], in_=dts2[o][p].ap[:],
                            func=AF.Exp,
                            scale=negA.ap[:, o * NST + n:o * NST + n + 1]),
                           ins=(dts2[oidx][par], negA),
                           outs=(dA2[oidx][npar],))
                        dbueng = ve if oidx == 0 else ge_
                        dbueng(lambda e, o=oidx, p=par, np_=npar:
                               e.tensor_tensor(
                                out=dBu2[o].ap[:], in0=dtu2[o][p].ap[:],
                                in1=Bsb[np_].ap[:], op=ALU.mult),
                               ins=(dtu2[oidx][par], Bsb[npar]),
                               outs=(dBu2[oidx],))
                        ve(lambda e, o=oidx, np_=npar, n=n:
                           e.tensor_tensor_scan(
                            out=hh2[o][np_].ap[:],
                            data0=dA2[o][np_].ap[:],
                            data1=dBu2[o].ap[:],
                            initial=stc2.ap[:, o * NST + n:
                                            o * NST + n + 1],
                            op0=ALU.mult, op1=ALU.add),
                           ins=(dA2[oidx][npar], dBu2[oidx], stc2),
                           outs=(hh2[oidx][npar],))
                        ve(lambda e, o=oidx, np_=npar, n=n: e.tensor_copy(
                            out=stc2.ap[:, o * NST + n:o * NST + n + 1],
                            in_=hh2[o][np_].ap[:, CWS - 1:CWS]),
                           ins=(hh2[oidx][npar],), outs=(stc2,))
                        if n == 0:
                            ge_(lambda e, o=oidx, np_=npar:
                                e.tensor_tensor(
                                 out=ycc[o].ap[:], in0=hh2[o][np_].ap[:],
                                 in1=Csb[np_].ap[:], op=ALU.mult),
                                ins=(hh2[oidx][npar], Csb[npar]),
                                outs=(ycc[oidx],))
                        else:
                            ge_(lambda e, o=oidx, np_=npar:
                                e.tensor_tensor(
                                 out=ytp[o].ap[:], in0=hh2[o][np_].ap[:],
                                 in1=Csb[np_].ap[:], op=ALU.mult),
                                ins=(hh2[oidx][npar], Csb[npar]),
                                outs=(ytp[oidx],))
                            ge_(lambda e, o=oidx: e.tensor_tensor(
                                 out=ycc[o].ap[:], in0=ycc[o].ap[:],
                                 in1=ytp[o].ap[:], op=ALU.add),
                                ins=(ytp[oidx], ycc[oidx]),
                                outs=(ycc[oidx],))
                for oidx in range(2):
                    se(lambda e, o=oidx: e.copy(out=ybc2[o].ap[:],
                                                in_=ycc[o].ap[:]),
                       ins=(ycc[oidx],), outs=(ybc2[oidx],))
                    dma("ge", a1b_in.ap[sl0, oidx][:, off:off + CWS],
                        ybc2[oidx].ap[:], ins=(ybc2[oidx],),
                        outs=(a1b_in,))
        AR.release()
        cc("AllToAll", a1b_in, a1b_out, rg=RG8)

        AR.mark()
        WOUT = [AR.alloc(16 * 128, BF16) for _ in range(NJ)]
        for j in range(NJ):
            wslab("W_out", j, 2048, WOUT[j])
        s1p = [AR.alloc(TC, BF16) for _ in range(NOT)]
        ybd = [AR.alloc(TC, BF16) for _ in range(2)]
        zb2d = [AR.alloc(TC, BF16) for _ in range(2)]
        ucb2d = [AR.alloc(TC, BF16) for _ in range(2)]
        szf = AR.alloc(TC, F32)
        y2 = AR.alloc(TC, F32)
        x1td = [AR.alloc(TC, F32) for _ in range(2)]
        xo = AR.alloc(TC, F32)
        for t in range(4):
            for ot in range(NOT):
                sl_, jj = ot // 2, ot % 2
                yb, zb2, ucb2 = ybd[ot % 2], zb2d[ot % 2], ucb2d[ot % 2]
                dma("sy", yb.ap[:],
                    a1b_out.ap[sl_, jj][:, t * TC:(t + 1) * TC],
                    ins=(a1b_out,), outs=(yb,))
                dma("sy", zb2.ap[:], sc_z.ap[ot, :, t * TC:(t + 1) * TC],
                    ins=(sc_z,), outs=(zb2,))
                dma("sy", ucb2.ap[:],
                    a1_in.ap[sl_, 1, jj][:, t * TC:(t + 1) * TC],
                    ins=(a1_in,), outs=(ucb2,))
                ve(lambda e, ot=ot, ucb2=ucb2, yb=yb:
                   e.scalar_tensor_tensor(
                    out=y2.ap[:], in0=ucb2.ap[:],
                    scalar=cols_t["Dp"].ap[:, ot:ot + 1], in1=yb.ap[:],
                    op0=ALU.mult, op1=ALU.add),
                   ins=(ucb2, cols_t["Dp"], yb), outs=(y2,))
                se(lambda e, zb2=zb2: e.activation(
                    out=szf.ap[:], in_=zb2.ap[:], func=AF.Silu),
                   ins=(zb2,), outs=(szf,))
                ve(lambda e, ot=ot: e.tensor_tensor(
                    out=s1p[ot].ap[:], in0=y2.ap[:], in1=szf.ap[:],
                    op=ALU.mult), ins=(y2, szf), outs=(s1p[ot],))
                if DBG and "d_y" in dbg_d:
                    dma("se", dbg_d["d_y"][ot * 128:(ot + 1) * 128,
                                           t * TC:(t + 1) * TC],
                        y2.ap[:], ins=(y2,))
            for j in range(NJ):
                x1t = x1td[j % 2]
                mm([WOUT[j].ap[:, kc * 128:(kc + 1) * 128]
                    for kc in range(16)],
                   [s1p[kc].ap[:] for kc in range(16)],
                   PS[4 + j % 2].ap[:, 0:TC], ins=[WOUT[j]] + s1p,
                   outs=(PS[4 + j % 2],))
                dma("sy", x1t.ap[:], sc_x1.ap[j, :, t * TC:(t + 1) * TC],
                    ins=(sc_x1,), outs=(x1t,))
                if DBG and "d_s1" in dbg_d:
                    AR.mark()
                    tq = AR.alloc(TC, F32)
                    ve(lambda e, tq=tq, ps=PS[4 + j % 2]: e.tensor_copy(
                        out=tq.ap[:], in_=ps.ap[:, 0:TC]),
                       ins=(PS[4 + j % 2],), outs=(tq,))
                    dma("se", dbg_d["d_s1"][j * 128:(j + 1) * 128,
                                            t * TC:(t + 1) * TC],
                        tq.ap[:], ins=(tq,))
                    AR.release()
                ve(lambda e, j=j, t=t, ps=PS[4 + j % 2]: e.tensor_tensor(
                    out=xo.ap[:], in0=ps.ap[:, 0:TC],
                    in1=G1[j].ap[:, t * TC:(t + 1) * TC], op=ALU.mult),
                   ins=(PS[4 + j % 2], G1[j]), outs=(xo,))
                ve(lambda e, x1t=x1t: e.tensor_tensor(
                    out=xo.ap[:], in0=xo.ap[:],
                    in1=x1t.ap[:], op=ALU.add),
                   ins=(xo, x1t), outs=(xo,))
                dma("se", xout_d[j * 128:(j + 1) * 128,
                                 t * TC:(t + 1) * TC], xo.ap[:],
                    ins=(xo,))
        AR.release()
        AR.release()
        final_toks = []
        for q in DMA_POOL:
            for s in DMA_POOL[q]:
                if p.cnt[s] > 0:
                    final_toks.append((s, p.cnt[s]))
        p.op("ge", lambda e: e.nop(), ins=final_toks, inc=False)

        with nc.Block() as block:
            @block.tensor
            def _(eng):
                _replay(p, "te", eng, sems)

            @block.vector
            def _(eng):
                _replay(p, "ve", eng, sems)

            @block.scalar
            def _(eng):
                _replay(p, "se", eng, sems)

            @block.gpsimd
            def _(eng):
                _replay(p, "ge", eng, sems)

            @block.sync
            def _(eng):
                _replay(p, "sy", eng, sems)

    return nc


_nc_cache = {}


def kernel(**inputs):
    T = np.asarray(inputs["L0_attn_x"]).shape[1] // 4
    if T not in _nc_cache:
        _nc_cache[T] = build_program(T)
    nc = _nc_cache[T]
    in_maps = setup_core_inputs(inputs, T)
    res = run_bass_kernel_spmd(nc, in_maps, core_ids=list(range(NCORES)))
    B = np.asarray(inputs["L0_attn_x"]).shape[0]
    L = np.asarray(inputs["L0_attn_x"]).shape[1]
    out = np.empty((B, L, D), np.float32)
    for c in range(NCORES):
        b, qq = c // 4, c % 4
        out[b, qq * T:(qq + 1) * T, :] = res.results[c]["xout"].T
    return out
